# version 03
# baseline (speedup 1.0000x reference)
import contextlib
import math
import numpy as np

import concourse.bass as bass
import concourse.mybir as mybir
from concourse.bass_utils import run_bass_kernel_spmd
from concourse.dve_op import DveOp
import concourse.dve_ops as dve_ops
from concourse.dve_spec import (
    Spec, Src0, Src1, C0, C1, C2, C3, scan, AluOp, minn, maxx, sq, lower,
)
from concourse.dve_table_gen import dve_ver_for

F32 = mybir.dt.float32
F16 = mybir.dt.float16
BF16 = mybir.dt.bfloat16
AF = mybir.ActivationFunctionType
OP = mybir.AluOpType
AX = mybir.AxisListType

MAGIC = float(1.5 * 2.0 ** 23)
D = 2048
T = 4096
DK = 128
HID = 8192
QMAX = 15.0
SHRINKS = [0.4 + 0.04 * i for i in range(16)]
EPS = 1e-5
ISQ = 1.0 / math.sqrt(DK)

DEBUG = False

_REGISTERED = {}


def _register_op(name, spec, reference):
    if name in _REGISTERED:
        return _REGISTERED[name]
    ver = dve_ver_for("TRN2")
    sha = lower(spec, ver=ver, perf_en=False, name=name).sha(ver)
    op = DveOp(name, spec, subdim=False, uops_sha={ver: sha}, reference=reference)
    dve_ops.OPS.append(op)
    row = dve_ops._CUSTOM_DVE_ROW_BASE + dve_ops.OPS.index(op)
    assert row < 0x20, "custom-DVE opcode rows exhausted"
    dve_ops._SUB_OPCODE_FOR_NAME[name] = row
    dve_ops.CUSTOM_DVE_SPECS[name] = spec
    dve_ops.CUSTOM_DVE_REFS[name] = reference
    _REGISTERED[name] = op
    return op


def _ref_qdq_search(in0, in1, s0, s1, imm2):
    t1 = (in0 * np.asarray(s0, in0.dtype)).astype(in0.dtype)
    rc = np.clip(np.round(t1), -QMAX, QMAX).astype(in0.dtype)
    d = (t1 - rc).astype(in0.dtype)
    e = (d * d).astype(in0.dtype)
    flat = e.reshape(e.shape[0], -1)
    return np.cumsum(flat, axis=-1, dtype=in0.dtype).reshape(e.shape)


def _ref_finaldeq(in0, in1, s0, s1, imm2):
    return np.clip(np.round(in0), -QMAX, QMAX).astype(in0.dtype)


def _finaldeq_op():
    a = Src0 + C0
    b = minn(a, C1)
    c = maxx(b, C3)
    return _register_op("ATH_FINALDEQ", Spec(body=c - C0), _ref_finaldeq)


def _qdq_search_op():
    t1 = Src0 * C0
    a = t1 + C1
    b = minn(a, C2)
    c = maxx(b, C3)
    r = c - C1
    d = t1 - r
    e = sq(d)
    return _register_op("ATH_QDQ_SEARCH", Spec(body=scan(AluOp.ADD, e)), _ref_qdq_search)



def _rope_tables():
    pairs = np.arange(DK // 2, dtype=np.float32)
    freqs = (1.0 / (10000.0 ** (2.0 * pairs / DK))).astype(np.float32)
    pos = np.arange(T, dtype=np.float32)
    ang = pos[None, :] * freqs[:, None]
    cos = np.cos(ang).astype(np.float32)
    sin = np.sin(ang).astype(np.float32)
    C2t = np.empty((128, T), np.float32)
    S2t = np.empty((128, T), np.float32)
    for p in range(128):
        g, slot = p // 32, p % 32
        i = g * 16 + (slot % 16)
        C2t[p] = cos[i]
        S2t[p] = (-sin[i]) if slot < 16 else sin[i]
    return C2t, S2t


def _perm128():
    d_of_p = np.empty(128, np.int64)
    for p in range(128):
        g, slot = p // 32, p % 32
        i = g * 16 + (slot % 16)
        d_of_p[p] = 2 * i + (0 if slot < 16 else 1)
    return d_of_p


def _tri_masks():
    import ml_dtypes
    out = np.zeros((4, 128, 512), ml_dtypes.bfloat16)
    for kt in range(4):
        for k in range(128):
            out[kt, k, kt * 128 + k:] = 1.0
    return out


def _prep_core_inputs(c, x, q_w, kv_w, c_w, w_fc, w_proj, C2t, S2t, perm, tri):
    b, r = c // 4, c % 4
    d = {}
    d["x_fullT"] = np.ascontiguousarray(x[b].T)
    d["x_ownT"] = np.ascontiguousarray(x[b, r * 1024:(r + 1) * 1024].T)
    rowsel = np.concatenate([512 * r + hh * 128 + perm for hh in range(4)])
    d["qwT"] = np.ascontiguousarray(q_w[rowsel, :].T.astype(np.float16))
    d["kwT"] = np.ascontiguousarray(kv_w[r * 128 + perm, :].T.astype(np.float16))
    d["vwT"] = np.ascontiguousarray(kv_w[512 + r * 128 + np.arange(128), :].T.astype(np.float16))
    d["cwT"] = np.ascontiguousarray(c_w[:, 512 * r:512 * (r + 1)].T.astype(np.float16))
    d["ropeC"] = C2t
    d["ropeS"] = S2t
    d["tri"] = tri
    d["wfc_sh"] = np.ascontiguousarray(w_fc[c * 1024:(c + 1) * 1024, :])
    d["wpj_sh"] = np.ascontiguousarray(w_proj[c * 256:(c + 1) * 256, :])
    shr = np.asarray(SHRINKS, np.float32)
    pat = np.empty((2, 128, 512), np.float32)
    pat[0] = np.tile(-(shr * shr), 32)[None, :]
    pat[1] = np.tile(np.arange(16, dtype=np.float32), 32)[None, :]
    d["qpat"] = pat
    return d



class Sem:
    def __init__(self, nc, stack, name):
        self.h = stack.enter_context(nc.semaphore(name))
        self.n = 0

    def inc(self, k=1):
        self.n += k
        return self.n


class Ring:

    def __init__(self, depth, sem):
        self.depth = depth
        self.sem = sem
        self.marks = []

    def guard(self, P, eng):
        i = len(self.marks)
        if i >= self.depth:
            m = self.marks[i - self.depth]
            if isinstance(m, tuple):
                P.wait(eng, m[0], m[1])
            else:
                P.wait(eng, self.sem, m)

    def mark(self, val=None, sem=None):
        if sem is not None:
            self.marks.append((sem, sem.n if val is None else val))
        else:
            self.marks.append(self.sem.n if val is None else val)

    def slot(self):
        return len(self.marks) % self.depth


class Prog:
    ENGINES = ("sync", "vector", "scalar", "tensor", "gpsimd")

    def __init__(self):
        self.q = {e: [] for e in self.ENGINES}

    def add(self, eng, fn):
        self.q[eng].append(fn)

    def wait(self, eng, sem, val=None):
        v = sem.n if val is None else val
        if v > 0:
            self.q[eng].append(lambda e, _s=sem.h, _v=v: e.wait_ge(_s, _v))

    def run(self, block):
        for eng in self.ENGINES:
            fns = self.q[eng]

            def body(e, _fns=fns):
                for f in _fns:
                    f(e)
            getattr(block, eng)(body)


SH_SWAP16 = [(j + 16) % 32 for j in range(32)]


def build_program(debug=DEBUG):
    _qdq_search_op()
    _finaldeq_op()
    QDQ_OP = _REGISTERED["ATH_QDQ_SEARCH"]
    FDQ_OP = _REGISTERED["ATH_FINALDEQ"]
    nc = bass.Bass("TRN2", target_bir_lowering=False, debug=False,
                   detect_race_conditions=False)

    x_fullT = nc.dram_tensor("x_fullT", [D, T], F32, kind="ExternalInput").ap()
    x_ownT = nc.dram_tensor("x_ownT", [D, 1024], F32, kind="ExternalInput").ap()
    qwT = nc.dram_tensor("qwT", [D, 512], F16, kind="ExternalInput").ap()
    kwT = nc.dram_tensor("kwT", [D, 128], F16, kind="ExternalInput").ap()
    vwT = nc.dram_tensor("vwT", [D, 128], F16, kind="ExternalInput").ap()
    cwT = nc.dram_tensor("cwT", [512, D], F16, kind="ExternalInput").ap()
    ropeC = nc.dram_tensor("ropeC", [128, T], F32, kind="ExternalInput").ap()
    ropeS = nc.dram_tensor("ropeS", [128, T], F32, kind="ExternalInput").ap()
    tri = nc.dram_tensor("tri", [4, 128, 512], BF16, kind="ExternalInput").ap()
    wfc_sh = nc.dram_tensor("wfc_sh", [1024, D], F32, kind="ExternalInput").ap()
    wpj_sh = nc.dram_tensor("wpj_sh", [256, HID], F32, kind="ExternalInput").ap()
    qpat = nc.dram_tensor("qpat", [2, 128, 512], F32, kind="ExternalInput").ap()

    out = nc.dram_tensor("out", [D, 1024], F32, kind="ExternalOutput").ap()

    spart_dram = nc.dram_tensor("spart_dram", [4, 16, 128, 1024], F16)
    sown_dram = nc.dram_tensor("sown_dram", [16, 128, 1024], F16)
    wqfc_b = nc.dram_tensor("wqfc_b", [1024, D], F16)
    wqfc_full = nc.dram_tensor("wqfc_full", [HID, D], F16)
    wqpj_b = nc.dram_tensor("wqpj_b", [256, HID], F16)
    wqpj_full = nc.dram_tensor("wqpj_full", [D, HID], F16)

    dbg = {}
    if debug:
        dbg["qh"] = nc.dram_tensor("dbg_qh", [128, 4 * T], F16, kind="ExternalOutput").ap()
        dbg["kh"] = nc.dram_tensor("dbg_kh", [128, T], F16, kind="ExternalOutput").ap()
        dbg["vt"] = nc.dram_tensor("dbg_vt", [128, T], BF16, kind="ExternalOutput").ap()
        dbg["xn"] = nc.dram_tensor("dbg_xn", [128, 16 * 1024], F16, kind="ExternalOutput").ap()
        dbg["mc"] = nc.dram_tensor("dbg_mc", [1, 8], F32, kind="ExternalOutput").ap()
        dbg["ah"] = nc.dram_tensor("dbg_ah", [128, 4 * T], F16, kind="ExternalOutput").ap()
        dbg["sown"] = nc.dram_tensor("dbg_sown", [16, 128, 1024], F16, kind="ExternalOutput").ap()
        dbg["wqfc"] = nc.dram_tensor("dbg_wqfc", [HID, D], F16, kind="ExternalOutput").ap()
        dbg["wqpj"] = nc.dram_tensor("dbg_wqpj", [D, HID], F16, kind="ExternalOutput").ap()
        dbg["spart"] = nc.dram_tensor("dbg_spart", [4, 16, 128, 1024], F16, kind="ExternalOutput").ap()

    stack = contextlib.ExitStack()
    sb = lambda name, shape, dt=F16: stack.enter_context(nc.sbuf_tensor(name, shape, dt))
    ps = lambda name, shape, dt=F32: stack.enter_context(nc.psum_tensor(name, shape, dt))
    S = lambda name: Sem(nc, stack, name)

    TT = 256
    NTT = T // TT

    qh = sb("qh", [128, 4 * T])
    kh = sb("kh", [128, T])
    vt = sb("vt", [128, T], BF16)
    xnow = sb("xnow", [128, 16 * 1024])
    arenaA = sb("arenaA", [128, 8192], F32)
    big1 = sb("big1", [128, 16384])
    nsqb = sb("nsqb", [128, 2 * TT])
    ropc = sb("ropc", [128, 2 * TT], F32)
    rops = sb("rops", [128, 2 * TT], F32)
    qwTs = sb("qwTs", [128, 16 * 512])
    kwTs = sb("kwTs", [128, 16 * 128])
    vwTs = sb("vwTs", [128, 16 * 128])
    srow = sb("srow", [1, 5 * TT], F32)
    mrow = sb("mrow", [1, 16], F32)
    nrow = sb("nrow", [1, 128], F32)
    mcore = sb("mcore", [128, 1], F32)
    rtmp1 = sb("rtmp1", [128, TT], F32)
    rtmp2 = sb("rtmp2", [128, TT], F32)
    rtmp3 = sb("rtmp3", [128, TT], F32)
    ones16 = sb("ones16", [128, 1])
    ones32 = sb("ones32", [128, 1], F32)
    onesrow32 = sb("onesrow32", [1, 128], F32)
    ones_bf = sb("ones_bf", [128, 1], BF16)
    trims = sb("trims", [128, 4 * 512], BF16)
    probsb = sb("probsb", [128, 3 * 512], BF16)
    sparth = sb("sparth", [128, 2 * 512])
    rdrow = sb("rdrow", [1, 512], F32)
    qsm = sb("qsm", [128, 8 * 32], F32)
    gatall = sb("gatall", [128, 512], F32)
    errsB = sb("errsB", [128, 512], F32)
    maskb = sb("maskb", [128, 512], mybir.dt.uint8)
    qpats = sb("qpats", [128, 2 * 512], F32)
    c3col = sb("c3col", [128, 1], F32)
    wqh = sb("wqh", [128, 2048])
    xf = arenaA
    xsq_v = big1[:, 0:16 * TT]
    xn_v = big1[:, 16 * TT:3 * 16 * TT]
    ah = big1

    pbank = [ps(f"pb{i}", [128, 512]) for i in range(8)]
    ps_stat = [pbank[0][0:1, :], pbank[1][0:1, :]]
    ps_bc = [pbank[2], pbank[3]]
    ps_proj = [pbank[4][:, 0:TT], pbank[5][:, 0:TT]]
    ps_nrm = [pbank[6][0:1, 0:TT], pbank[7][0:1, 0:TT]]

    sem_in = S("sem_in")
    sem_xp = [S("sem_x0"), S("sem_x1")]
    sem_rop = [S("sem_ro0"), S("sem_ro1")]
    sem_sq = S("sem_sq")
    sem_mmst = S("sem_mmst")
    sem_st1 = S("sem_st1")
    sem_sd = S("sem_sd")
    sem_rst = S("sem_rst")
    sem_bc = S("sem_bc")
    sem_xn = S("sem_xn")
    sem_qp = S("sem_qp")
    sem_rp = S("sem_rp")
    sem_nrm = S("sem_nrm")
    sem_nev = S("sem_nev")
    sem_mc = S("sem_mc")
    sem_out = S("sem_out")
    sem_qk = S("sem_qk")
    sem_pr = S("sem_pr")
    sem_pm = S("sem_pm")
    sem_pv = S("sem_pv")
    sem_rd = S("sem_rd")
    sem_bc2 = S("sem_bc2")
    sem_pt = S("sem_pt")
    sem_ev = S("sem_ev")
    sem_cp = S("sem_cp")
    sem_ce = S("sem_ce")
    sem_sp = S("sem_sp")
    sem_cw = S("sem_cw")
    sem_cc = S("sem_cc")
    sem_wtp = [S("sem_wt0"), S("sem_wt1")]
    sem_sxp = [S("sem_sx0"), S("sem_sx1")]
    sem_outp = [S("sem_out0"), S("sem_out1")]
    sem_spp = [S("sem_spp0"), S("sem_spp1")]

    P = Prog()

    def dma(eng, dst, src, sem):
        P.add(eng, lambda e, _d=dst, _s=src: e.dma_start(out=_d, in_=_s).then_inc(sem.h, 16))
        return sem.inc(16)

    r_xf = Ring(2, sem_xn)
    r_rope = Ring(2, sem_rp)
    r_xsq = Ring(1, sem_mmst)
    r_xn = Ring(2, sem_qp)
    r_nsqb = Ring(2, sem_nrm)
    r_stat = Ring(2, sem_st1)
    r_bc = Ring(2, sem_xn)
    r_proj = Ring(2, sem_rp)
    r_nrm = Ring(2, sem_nev)
    r_srow = Ring(1, sem_bc)

    dma("sync", qwTs.ap().rearrange("p (c q) -> p c q", c=16),
        qwT.rearrange("(c p) q -> p c q", p=128), sem_in)
    dma("sync", kwTs.ap().rearrange("p (c q) -> p c q", c=16),
        kwT.rearrange("(c p) q -> p c q", p=128), sem_in)
    dma("sync", vwTs.ap().rearrange("p (c q) -> p c q", c=16),
        vwT.rearrange("(c p) q -> p c q", p=128), sem_in)
    IN_W = sem_in.n

    P.add("vector", lambda e: e.memset(ones16[:, :], 1.0))
    P.add("vector", lambda e: e.memset(ones32[:, :], 1.0))
    P.add("vector", lambda e: e.memset(onesrow32[:, :], 1.0))
    P.add("vector", lambda e: e.memset(ones_bf[:, :], 1.0))
    P.add("vector", lambda e: e.memset(mrow[:, :], 0.0).then_inc(sem_st1.h, 1))
    CONSTS_RDY = sem_st1.inc(1)
    P.wait("tensor", sem_st1, CONSTS_RDY)

    def ln_tile(i, pass_own):
        par = i % 2
        xfi = xf[:, par * 16 * TT:(par + 1) * 16 * TT].rearrange("p (c t) -> p c t", c=16)
        xsqi = xsq_v.rearrange("p (c t) -> p c t", c=16)
        xni = xn_v[:, par * 16 * TT:(par + 1) * 16 * TT].rearrange("p (c t) -> p c t", c=16)
        tok0 = i * TT
        xin = x_ownT if pass_own else x_fullT

        r_xf.guard(P, "sync")
        xld = dma("sync", xfi, xin.rearrange("(c p) t -> p c t", p=128)[:, :, tok0:tok0 + TT], sem_xp[par])
        if not pass_own:
            r_rope.guard(P, "sync")
            dma("sync", ropc[:, par * TT:(par + 1) * TT], ropeC[:, tok0:tok0 + TT], sem_rop[par])
            rld = dma("sync", rops[:, par * TT:(par + 1) * TT], ropeS[:, tok0:tok0 + TT], sem_rop[par])

        P.wait("scalar", sem_xp[par], xld)
        r_xsq.guard(P, "scalar")
        for c in range(16):
            last = c == 15
            P.add("scalar", lambda e, _c=c, _l=last: (
                e.activation(xsqi[:, _c, :], xfi[:, _c, :], AF.Square).then_inc(sem_sq.h, 1)
                if _l else e.activation(xsqi[:, _c, :], xfi[:, _c, :], AF.Square)))
        sqd = sem_sq.inc(1)

        P.wait("tensor", sem_xp[par], xld)
        r_stat.guard(P, "tensor")
        pst = ps_stat[par]
        for c in range(16):
            P.add("tensor", lambda e, _c=c: e.matmul(
                pst[:, 0:TT], ones32[:, :], xfi[:, _c, :], start=(_c == 0), stop=(_c == 15)))
        P.wait("tensor", sem_sq, sqd)
        for c in range(16):
            last = c == 15
            P.add("tensor", lambda e, _c=c, _l=last: (
                e.matmul(pst[:, TT:2 * TT], ones16[:, :], xsqi[:, _c, :],
                         start=(_c == 0), stop=(_c == 15)).then_inc(sem_mmst.h, 1)
                if _l else
                e.matmul(pst[:, TT:2 * TT], ones16[:, :], xsqi[:, _c, :],
                         start=(_c == 0), stop=(_c == 15))))
        mmst = sem_mmst.inc(1)
        r_xsq.mark()
        r_xf_partial_done = mmst

        P.wait("vector", sem_mmst, mmst)
        r_srow.guard(P, "vector")
        P.add("vector", lambda e: e.tensor_scalar(
            out=srow[:, 0:TT], in0=pst[:, 0:TT], scalar1=1.0 / D, scalar2=None, op0=OP.mult))
        P.add("vector", lambda e: e.tensor_scalar(
            out=srow[:, TT:2 * TT], in0=pst[:, TT:2 * TT], scalar1=1.0 / D, scalar2=EPS,
            op0=OP.mult, op1=OP.add))
        P.add("vector", lambda e: e.tensor_mul(srow[:, 2 * TT:3 * TT], srow[:, 0:TT], srow[:, 0:TT]))
        P.add("vector", lambda e: e.tensor_sub(
            srow[:, TT:2 * TT], srow[:, TT:2 * TT], srow[:, 2 * TT:3 * TT]).then_inc(sem_st1.h, 1))
        st1 = sem_st1.inc(1)
        r_stat.mark()

        P.wait("scalar", sem_st1, st1)
        P.add("scalar", lambda e: e.activation(
            srow[:, 2 * TT:3 * TT], srow[:, TT:2 * TT], AF.Sqrt).then_inc(sem_sd.h, 1))
        sdd = sem_sd.inc(1)

        P.wait("vector", sem_sd, sdd)
        P.add("vector", lambda e: e.reciprocal(srow[:, 3 * TT:4 * TT], srow[:, 2 * TT:3 * TT]))
        P.add("vector", lambda e: e.tensor_mul(
            srow[:, 4 * TT:5 * TT], srow[:, 0:TT], srow[:, 3 * TT:4 * TT]).then_inc(sem_rst.h, 1))
        rst = sem_rst.inc(1)

        P.wait("tensor", sem_rst, rst)
        r_bc.guard(P, "tensor")
        pbc = ps_bc[par]
        P.add("tensor", lambda e: e.matmul(
            pbc[:, 0:TT], onesrow32[:, :], srow[:, 3 * TT:4 * TT], start=True, stop=True))
        P.add("tensor", lambda e: e.matmul(
            pbc[:, TT:2 * TT], onesrow32[:, :], srow[:, 4 * TT:5 * TT],
            start=True, stop=True).then_inc(sem_bc.h, 1))
        bcd = sem_bc.inc(1)
        r_srow.mark()

        P.wait("vector", sem_bc, bcd)
        r_xn.guard(P, "vector")
        for c in range(16):
            last = c == 15
            P.add("vector", lambda e, _c=c: e.tensor_mul(rtmp3[:, :], xfi[:, _c, :], pbc[:, 0:TT]))
            P.add("vector", lambda e, _c=c, _l=last: (
                e.tensor_sub(xni[:, _c, :], rtmp3[:, :], pbc[:, TT:2 * TT]).then_inc(sem_xn.h, 1)
                if _l else e.tensor_sub(xni[:, _c, :], rtmp3[:, :], pbc[:, TT:2 * TT])))
        if pass_own:
            for c in range(16):
                P.add("vector", lambda e, _c=c: e.tensor_copy(
                    out=xnow[:, _c * 1024 + tok0:_c * 1024 + tok0 + TT], in_=xni[:, _c, :]))
        P.add("vector", lambda e: e.engine_nop().then_inc(sem_xn.h, 1))
        xnd = sem_xn.inc(2)
        r_xf.mark(xnd)
        r_bc.mark(xnd)

        if pass_own:
            r_xn.mark(sem_qp.n)
            r_rope.mark(sem_rp.n)
            return

        if i == 0:
            P.wait("tensor", sem_in, IN_W)
        P.wait("tensor", sem_xn, xnd)
        for u in range(5):
            pp = ps_proj[r_proj.slot()]
            r_proj.guard(P, "tensor")
            for c in range(16):
                if u < 4:
                    lhs = qwTs[:, c * 512 + u * 128: c * 512 + u * 128 + 128]
                else:
                    lhs = kwTs[:, c * 128:(c + 1) * 128]
                last = c == 15
                P.add("tensor", lambda e, _l=lhs, _c=c, _last=last, _pp=pp: (
                    e.matmul(_pp[:, :], _l, xni[:, _c, :],
                             start=(_c == 0), stop=(_c == 15)).then_inc(sem_qp.h, 1)
                    if _last else
                    e.matmul(_pp[:, :], _l, xni[:, _c, :], start=(_c == 0), stop=(_c == 15))))
            qpd = sem_qp.inc(1)

            P.wait("vector", sem_qp, qpd)
            if u == 0:
                P.wait("vector", sem_rop[par], rld)
            dst = qh[:, u * T + tok0: u * T + tok0 + TT] if u < 4 else kh[:, tok0:tok0 + TT]
            P.add("vector", lambda e, _pp=pp: e.tensor_copy(out=rtmp3[:, :], in_=_pp[:, :]))
            P.add("vector", lambda e: e.stream_shuffle(out=rtmp1[:, :], in_=rtmp3[:, :], mask=SH_SWAP16))
            P.add("vector", lambda e: e.tensor_mul(
                rtmp2[:, :], rtmp3[:, :], ropc[:, par * TT:(par + 1) * TT]))
            P.add("vector", lambda e: e.tensor_mul(
                rtmp1[:, :], rtmp1[:, :], rops[:, par * TT:(par + 1) * TT]))
            P.add("vector", lambda e, _d=dst: e.tensor_add(
                _d, rtmp2[:, :], rtmp1[:, :]).then_inc(sem_rp.h, 1))
            rpd = sem_rp.inc(1)
            r_proj.mark(rpd)
            if u == 4:
                r_rope.mark(rpd)

            P.wait("scalar", sem_rp, rpd)
            nslot = r_nsqb.slot()
            r_nsqb.guard(P, "scalar")
            nb = nsqb[:, nslot * TT:(nslot + 1) * TT]
            P.add("scalar", lambda e, _d=dst, _n=nb: e.activation(
                _n, _d, AF.Square).then_inc(sem_sq.h, 1))
            nsq = sem_sq.inc(1)

            P.wait("tensor", sem_sq, nsq)
            pn = ps_nrm[r_nrm.slot()]
            r_nrm.guard(P, "tensor")
            P.add("tensor", lambda e, _n=nb, _p=pn: e.matmul(
                _p[:, :], ones16[:, :], _n, start=True, stop=True).then_inc(sem_nrm.h, 1))
            nmd = sem_nrm.inc(1)
            r_nsqb.mark(nmd)

            P.wait("vector", sem_nrm, nmd)
            idx = (i * 4 + u) if u < 4 else (64 + i)
            P.add("vector", lambda e, _p=pn, _ix=idx: e.reduce_max(
                nrow[:, _ix:_ix + 1], _p[:, :], axis=AX.X).then_inc(sem_nev.h, 1))
            nev = sem_nev.inc(1)
            r_nrm.mark(nev)

        for v in range(TT // 128):
            pp = ps_proj[r_proj.slot()]
            r_proj.guard(P, "tensor")
            for c in range(16):
                last = c == 15
                P.add("tensor", lambda e, _c=c, _v=v, _last=last, _pp=pp: (
                    e.matmul(_pp[:, 0:128], xni[:, _c, _v * 128:(_v + 1) * 128],
                             vwTs[:, _c * 128:(_c + 1) * 128],
                             start=(_c == 0), stop=(_c == 15)).then_inc(sem_qp.h, 1)
                    if _last else
                    e.matmul(_pp[:, 0:128], xni[:, _c, _v * 128:(_v + 1) * 128],
                             vwTs[:, _c * 128:(_c + 1) * 128],
                             start=(_c == 0), stop=(_c == 15))))
            qpd = sem_qp.inc(1)
            P.wait("scalar", sem_qp, qpd)
            ktile = (tok0 + v * 128) // 128
            P.add("scalar", lambda e, _k=ktile, _pp=pp: e.activation(
                vt[:, _k * 128:(_k + 1) * 128], _pp[:, 0:128], AF.Copy).then_inc(sem_rp.h, 1))
            rpd = sem_rp.inc(1)
            r_proj.mark(rpd)

        r_xn.mark(qpd)

    for i in range(NTT):
        ln_tile(i, pass_own=False)

    P.wait("vector", sem_nev, sem_nev.n)
    P.add("vector", lambda e: e.tensor_copy(out=mrow[:, 8:10], in_=mrow[:, 8:10]))
    P.add("vector", lambda e: e.reduce_max(mrow[:, 0:1], nrow[:, 0:64], axis=AX.X))
    P.add("vector", lambda e: e.reduce_max(mrow[:, 1:2], nrow[:, 64:80], axis=AX.X))
    P.add("vector", lambda e: e.tensor_copy(out=mrow[:, 8:10], in_=mrow[:, 8:10]))
    P.add("vector", lambda e: e.tensor_mul(
        mrow[:, 2:3], mrow[:, 0:1], mrow[:, 1:2]).then_inc(sem_st1.h, 1))
    st1 = sem_st1.inc(1)
    P.wait("scalar", sem_st1, st1)
    P.add("scalar", lambda e: e.activation(
        mrow[:, 3:4], mrow[:, 2:3], AF.Sqrt).then_inc(sem_sd.h, 1))
    sdd = sem_sd.inc(1)
    P.wait("tensor", sem_sd, sdd)
    P.add("tensor", lambda e: e.matmul(
        ps_bc[0][:, 0:1], onesrow32[:, :], mrow[:, 3:4], start=True, stop=True).then_inc(sem_nrm.h, 1))
    nmd = sem_nrm.inc(1)
    P.wait("vector", sem_nrm, nmd)
    P.add("vector", lambda e: e.tensor_scalar(
        out=mcore[:, :], in0=ps_bc[0][:, 0:1], scalar1=-ISQ, scalar2=None,
        op0=OP.mult).then_inc(sem_mc.h, 1))
    sem_mc.inc(1)

    P.wait("tensor", sem_mc, 1)
    P.wait("sync", sem_qp, sem_qp.n)
    dma("sync", qwTs.ap().rearrange("p (c o) -> p c o", c=4),
        cwT.rearrange("(c p) o -> p c o", p=128), sem_cw)
    CW_T = sem_cw.n
    dma("sync", trims.ap().rearrange("p (k q) -> p k q", k=4),
        tri.transpose([1, 0, 2]), sem_cw)
    TRI_T = sem_cw.n
    for i in range(1024 // TT):
        ln_tile(i, pass_own=True)
    cws = qwTs

    for _eng in ("tensor", "vector", "scalar"):
        for _sem in (sem_st1, sem_rst, sem_xn, sem_rp, sem_nev, sem_mc, sem_sq,
                     sem_mmst, sem_bc, sem_qp, sem_nrm, sem_in, sem_xp[0], sem_xp[1],
                     sem_rop[0], sem_rop[1], sem_sd):
            P.wait(_eng, _sem, _sem.n)
        P.wait(_eng, sem_cw, TRI_T)

    r_lg = Ring(2, sem_pr)
    pvs_t = []
    gexp = [0]
    r_pv = Ring(2, sem_pt)
    r_sums = Ring(2, sem_rd)
    r_rdx = Ring(2, sem_ev)
    pvtmp = ropc
    r_pvt = Ring(1, sem_ev)
    r_rdr = Ring(1, sem_bc2)
    r_sph = Ring(2, sem_sp)

    unit = 0
    for qc in range(8):
        for h in range(4):
            nkt = 4 * qc + 4
            pvslot = unit % 2
            pv = pbank[2 + pvslot]
            sums = pbank[4 + pvslot][0:1, :]
            qslice = qh[:, h * T + qc * 512: h * T + (qc + 1) * 512]

            r_pv.guard(P, "tensor")
            r_sums.guard(P, "tensor")

            exp_t = []
            msk_t = []

            def emit_pvs(kt, _pv=pv, _sums=sums, _nkt=nkt, _qc=qc):
                P.wait("tensor", sem_pr, exp_t[kt])
                if msk_t[kt] is not None:
                    P.wait("tensor", sem_pm, msk_t[kt])
                pslot = prob_slots[kt]
                pb = probsb[:, pslot * 512:(pslot + 1) * 512]
                P.add("tensor", lambda e, _kt=kt, _pb=pb: e.matmul(
                    _pv[:, :], vt[:, _kt * 128:(_kt + 1) * 128], _pb,
                    start=(_kt == 0), stop=(_kt == _nkt - 1)))
                P.add("tensor", lambda e, _kt=kt, _pb=pb: e.matmul(
                    _sums, ones_bf[:, :], _pb,
                    start=(_kt == 0), stop=(_kt == _nkt - 1)).then_inc(sem_pv.h, 1))
                sem_pv.inc(1)
                pvs_t.append(sem_pv.n)

            prob_slots = []
            for kt in range(nkt):
                lslot = r_lg.slot()
                lg = pbank[lslot]
                r_lg.guard(P, "tensor")
                P.add("tensor", lambda e, _kt=kt, _lg=lg, _q=qslice: e.matmul(
                    _lg[:, :], kh[:, _kt * 128:(_kt + 1) * 128], _q,
                    start=True, stop=True).then_inc(sem_qk.h, 1))
                qkt = sem_qk.inc(1)

                P.wait("scalar", sem_qk, qkt)
                g = gexp[0]
                gexp[0] += 1
                pslot = g % 3
                prob_slots.append(pslot)
                if g >= 3:
                    P.wait("scalar", sem_pv, pvs_t[g - 3])
                pb = probsb[:, pslot * 512:(pslot + 1) * 512]
                P.add("scalar", lambda e, _lg=lg, _pb=pb: e.activation(
                    _pb, _lg[:, :], AF.Exp, bias=mcore[:, :], scale=ISQ).then_inc(sem_pr.h, 1))
                exp_t.append(sem_pr.inc(1))
                r_lg.mark()

                if kt >= 4 * qc:
                    P.wait("vector", sem_pr, exp_t[-1])
                    dk_ = kt - 4 * qc
                    P.add("vector", lambda e, _pb=pb, _dk=dk_: e.tensor_mul(
                        _pb, _pb, trims[:, _dk * 512:(_dk + 1) * 512]).then_inc(sem_pm.h, 1))
                    msk_t.append(sem_pm.inc(1))
                else:
                    msk_t.append(None)

                if kt >= 1:
                    emit_pvs(kt - 1)
            emit_pvs(nkt - 1)
            pv_done = sem_pv.n

            P.wait("vector", sem_pv, pv_done)
            r_rdr.guard(P, "vector")
            rrow = rdrow[:, 0:512]
            P.add("vector", lambda e, _s=sums, _r=rrow: e.reciprocal_approx_fast(_r, _s).then_inc(sem_rd.h, 1))
            rd_t = sem_rd.inc(1)
            r_sums.mark(rd_t)

            P.wait("tensor", sem_rd, rd_t)
            rdx = pbank[6 + pvslot]
            r_rdx.guard(P, "tensor")
            P.add("tensor", lambda e, _r=rrow, _o=rdx: e.matmul(
                _o[:, :], onesrow32[:, :], _r, start=True, stop=True).then_inc(sem_bc2.h, 1))
            bc2_t = sem_bc2.inc(1)
            r_rdr.mark(bc2_t)

            P.wait("scalar", sem_pv, pv_done)
            r_pvt.guard(P, "scalar")
            pvt = pvtmp[:, 0:512]
            P.add("scalar", lambda e, _pv=pv, _t=pvt: e.activation(
                _t, _pv[:, :], AF.Copy).then_inc(sem_pt.h, 1))
            pt_t = sem_pt.inc(1)
            r_pv.mark(pt_t)

            P.wait("vector", sem_pt, pt_t)
            P.wait("vector", sem_bc2, bc2_t)
            ahsl = ah[:, h * T + qc * 512: h * T + (qc + 1) * 512]
            P.add("vector", lambda e, _t=pvt, _x=rdx, _a=ahsl: e.tensor_mul(
                _a, _t, _x[:, :]).then_inc(sem_ev.h, 1))
            ev_t = sem_ev.inc(1)
            r_rdx.mark(ev_t)
            r_pvt.mark(ev_t)
            unit += 1

        P.wait("tensor", sem_ev, sem_ev.n)
        if qc == 0:
            P.wait("tensor", sem_cw, TRI_T)
        for ot in range(16):
            cslot = r_lg.slot()
            cps = pbank[cslot]
            r_lg.guard(P, "tensor")
            for fh in range(4):
                last = fh == 3
                P.add("tensor", lambda e, _f=fh, _o=ot, _c=cps, _qc=qc, _l=last: (
                    e.matmul(_c[:, :], cws[:, _f * 2048 + _o * 128: _f * 2048 + _o * 128 + 128],
                             ah[:, _f * T + _qc * 512: _f * T + (_qc + 1) * 512],
                             start=(_f == 0), stop=(_f == 3)).then_inc(sem_cp.h, 1)
                    if _l else
                    e.matmul(_c[:, :], cws[:, _f * 2048 + _o * 128: _f * 2048 + _o * 128 + 128],
                             ah[:, _f * T + _qc * 512: _f * T + (_qc + 1) * 512],
                             start=(_f == 0), stop=(_f == 3))))
            cp_t = sem_cp.inc(1)

            P.wait("vector", sem_cp, cp_t)
            sslot = r_sph.slot()
            r_sph.guard(P, "vector")
            sph = sparth[:, sslot * 512:(sslot + 1) * 512]
            P.add("vector", lambda e, _c=cps, _s=sph: e.tensor_copy(
                out=_s, in_=_c[:, :]).then_inc(sem_ce.h, 1))
            ce_t = sem_ce.inc(1)
            r_lg.mark(ce_t, sem=sem_ce)

            P.wait("gpsimd", sem_ce, ce_t)
            dst = spart_dram.ap()[qc // 2, ot, :, (qc % 2) * 512:(qc % 2) * 512 + 512]
            _ssp = sem_spp[sslot]
            P.add("gpsimd", lambda e, _d=dst, _s=sph, _m=_ssp: e.dma_start(
                out=_d, in_=_s).then_inc(_m.h, 16))
            sp_t = _ssp.inc(16)
            r_sph.mark(sp_t, sem=_ssp)

    sem_qw = S("sem_qw")
    sem_qu = S("sem_qu")
    sem_qv = S("sem_qv")
    sem_qs = S("sem_qs")
    sem_qp2 = S("sem_qp2")

    P.wait("gpsimd", sem_spp[0], sem_spp[0].n)
    P.wait("gpsimd", sem_spp[1], sem_spp[1].n)
    P.add("gpsimd", lambda e: e.collective_compute(
        "ReduceScatter", OP.add,
        replica_groups=[[0, 1, 2, 3], [4, 5, 6, 7]],
        ins=[spart_dram.ap().opt()], outs=[sown_dram.ap().opt()],
    ).then_inc(sem_cc.h, 1))
    RS_T = sem_cc.inc(1)

    dma("sync", qpats.ap().rearrange("p (k q) -> p k q", k=2), qpat.transpose([1, 0, 2]), sem_qp2)
    P.wait("vector", sem_qp2, sem_qp2.n)
    P.add("vector", lambda e: e.memset(c3col[:, :], MAGIC - 15.0))

    wv = arenaA[:, 0:2048]
    uv = arenaA[:, 2048:4096]
    prefs = [arenaA[:, 4096:6144], arenaA[:, 6144:8192]]
    AM, BA, IB, SI, SS, IV, SB2 = [qsm[:, i * 32:(i + 1) * 32] for i in range(7)]
    junk = qsm[:, 7 * 32:8 * 32]

    P.wait("sync", sem_xn, sem_xn.n)
    r_qw = Ring(1, sem_qu)
    r_wqh = Ring(1, sem_qs)

    def g3(t, n=64):
        return t.rearrange("p (b k) -> p b k", k=n)

    for qu in range(16):
        if qu < 8:
            src = wfc_sh[qu * 128:(qu + 1) * 128, :]
        else:
            t8, ch = (qu - 8) // 4, (qu - 8) % 4
            src = wpj_sh[t8 * 128:(t8 + 1) * 128, ch * 2048:(ch + 1) * 2048]
        r_qw.guard(P, "sync")
        qw_t = dma("sync", wv, src, sem_qw)

        P.wait("vector", sem_qw, qw_t)
        P.add("vector", lambda e: e.reduce_max(AM, g3(wv), axis=AX.X, apply_absolute_value=True))
        P.add("vector", lambda e: e.tensor_copy(out=junk, in_=junk))
        P.add("vector", lambda e: e.tensor_scalar(
            out=BA, in0=AM, scalar1=1e-8, scalar2=1.0 / QMAX, op0=OP.max, op1=OP.mult))
        P.add("vector", lambda e: e.tensor_copy(out=junk, in_=junk))
        P.add("vector", lambda e: e.reciprocal(IB, BA))
        P.add("vector", lambda e: e.tensor_copy(out=junk, in_=junk))
        P.add("vector", lambda e: e.tensor_mul(
            g3(uv), g3(wv), IB.unsqueeze(2).broadcast_to([128, 32, 64])).then_inc(sem_qu.h, 1))
        r_qw.mark(sem_qu.inc(1))
        P.add("vector", lambda e: e.tensor_copy(out=junk, in_=junk))

        for si in range(16):
            pf = prefs[si % 2]
            P.add("vector", lambda e, _si=si, _pf=pf: e._custom_dve(
                QDQ_OP, out=_pf, in0=uv, in1=c3col[:, :],
                s0=float(1.0 / SHRINKS[_si]), s1=MAGIC, imm2=MAGIC + 15.0))
            if si >= 1:
                pprev = prefs[(si - 1) % 2]
                P.add("vector", lambda e, _si=si, _p=pprev: e.tensor_copy(
                    out=gatall[:, (_si - 1) * 32:_si * 32], in_=g3(_p)[:, :, 63]))
        P.add("vector", lambda e: e.tensor_copy(out=junk, in_=junk))
        P.add("vector", lambda e: e.tensor_copy(
            out=gatall[:, 15 * 32:16 * 32], in_=g3(prefs[1])[:, :, 63]))

        gat3 = gatall.ap().rearrange("p (s b) -> p s b", s=16)
        eview = errsB.ap().rearrange("p (b s) -> p b s", s=16).transpose([0, 2, 1])
        VG = lambda: P.add("vector", lambda e: e.tensor_copy(out=junk, in_=junk))
        VG()
        P.add("vector", lambda e: e.tensor_sub(eview[:, :, 1:32], gat3[:, :, 1:32], gat3[:, :, 0:31]))
        P.add("vector", lambda e: e.tensor_copy(out=eview[:, :, 0:1], in_=gat3[:, :, 0:1]))
        VG()
        P.add("vector", lambda e: e.tensor_mul(errsB[:, :], errsB[:, :], qpats[:, 0:512]))
        eb3 = errsB.ap().rearrange("p (b s) -> p b s", s=16)
        VG()
        P.add("vector", lambda e: e.reduce_max(SI, eb3, axis=AX.X))
        VG()
        P.add("vector", lambda e: e.tensor_tensor(
            maskb.ap().rearrange("p (b s) -> p b s", s=16), eb3,
            SI.unsqueeze(2).broadcast_to([128, 32, 16]), OP.is_ge))
        P.add("vector", lambda e: e.memset(gatall[:, :], 1e9))
        VG()
        P.add("vector", lambda e: e.copy_predicated(gatall[:, :], maskb[:, :], qpats[:, 512:1024]))
        VG()
        P.add("vector", lambda e: e.tensor_reduce(
            SI, gatall.ap().rearrange("p (b s) -> p b s", s=16), axis=AX.X, op=OP.min))
        VG()
        P.add("vector", lambda e: e.tensor_scalar(
            out=SS, in0=SI, scalar1=0.04, scalar2=0.4, op0=OP.mult, op1=OP.add))
        P.add("vector", lambda e: e.tensor_copy(out=junk, in_=junk))
        P.add("vector", lambda e: e.reciprocal(IV, SS))
        P.add("vector", lambda e: e.tensor_mul(SB2, SS, BA))
        r_wqh.guard(P, "vector")
        wqv = wqh[:, 0:2048]
        P.add("vector", lambda e: e.tensor_mul(
            g3(prefs[1]), g3(uv), IV.unsqueeze(2).broadcast_to([128, 32, 64])))
        P.add("vector", lambda e: e.tensor_copy(out=junk, in_=junk))
        P.add("vector", lambda e: e._custom_dve(
            FDQ_OP, out=prefs[0], in0=prefs[1], in1=c3col[:, :],
            s0=MAGIC, s1=MAGIC + 15.0))
        P.add("vector", lambda e: e.tensor_copy(out=junk, in_=junk))
        P.add("vector", lambda e: e.tensor_mul(
            g3(wqv), g3(prefs[0]),
            SB2.unsqueeze(2).broadcast_to([128, 32, 64])).then_inc(sem_qv.h, 1))
        qv_t = sem_qv.inc(1)

        P.wait("gpsimd", sem_qv, qv_t)
        if qu < 8:
            dst = wqfc_b.ap()[qu * 128:(qu + 1) * 128, :]
        else:
            t8, ch = (qu - 8) // 4, (qu - 8) % 4
            dst = wqpj_b.ap()[t8 * 128:(t8 + 1) * 128, ch * 2048:(ch + 1) * 2048]
        qs_t = dma("gpsimd", dst, wqv, sem_qs)
        r_wqh.mark(qs_t)

        if qu == 7:
            P.wait("gpsimd", sem_qs, sem_qs.n)
            P.add("gpsimd", lambda e: e.collective_compute(
                "AllGather", OP.bypass, replica_groups=[list(range(8))],
                ins=[wqfc_b.ap().opt()], outs=[wqfc_full.ap().opt()],
            ).then_inc(sem_cc.h, 1))
            AGFC_T = sem_cc.inc(1)

    P.wait("gpsimd", sem_qs, sem_qs.n)
    P.add("gpsimd", lambda e: e.collective_compute(
        "AllGather", OP.bypass, replica_groups=[list(range(8))],
        ins=[wqpj_b.ap().opt()], outs=[wqpj_full.ap().opt()],
    ).then_inc(sem_cc.h, 1))
    AGPJ_T = sem_cc.inc(1)


    sem_fcd = S("sem_fcd")
    sem_fq = S("sem_fq")
    sem_hv = S("sem_hv")
    sem_pj = S("sem_pj")
    sem_g = S("sem_g")
    sem_go = S("sem_go")

    wtb = kwTs
    relub = vwTs
    sb_s = qwTs
    sb_x = arenaA[:, 0:2048]
    outb = arenaA[:, 2048:4096]

    def hsq_slice(ht, th):
        if ht < 32:
            return big1[:, ht * 512:(ht + 1) * 512]
        return qh[:, (ht - 32) * 512:(ht - 31) * 512]

    r_wt = Ring(2, sem_fcd)
    r_fcp = Ring(4, sem_hv)
    r_pjp = Ring(4, sem_g)
    r_relu = Ring(2, sem_hv)
    r_gb = Ring(2, sem_go)
    r_sx = Ring(2, sem_go)
    r_outb = Ring(2, sem_out)

    for _eng in ("tensor", "vector", "scalar", "sync"):
        for _sem in (sem_pr, sem_pm, sem_pv, sem_rd, sem_bc2, sem_pt, sem_ev, sem_cp,
                     sem_ce, sem_qv, sem_qs, sem_qu, sem_qw, sem_spp[0], sem_spp[1]):
            P.wait(_eng, _sem, _sem.n)
    P.wait("tensor", sem_cc, AGFC_T)
    P.wait("sync", sem_cc, AGFC_T)

    for th in range(2):
        for hg in range(16):
            for dc in range(16):
                wslot = r_wt.slot()
                r_wt.guard(P, "sync")
                wtile = wtb[:, wslot * 512:(wslot + 1) * 512]
                _sw = sem_wtp[wslot]
                P.add("sync", lambda e, _hg=hg, _dc=dc, _w=wtile, _s=_sw: e.dma_start(
                    out=_w, in_=wqfc_full.ap()[_hg * 512:(_hg + 1) * 512, _dc * 128:(_dc + 1) * 128],
                    transpose=True).then_inc(_s.h, 16))
                wt_t = _sw.inc(16)

                P.wait("tensor", _sw, wt_t)
                if dc == 0 and hg >= 1:
                    P.wait("tensor", sem_hv, r_fcp.marks[-1])
                for j in range(4):
                    last = (j == 3)
                    P.add("tensor", lambda e, _j=j, _dc=dc, _th=th, _w=wtile, _l=last: (
                        e.matmul(pbank[_j][:, :], _w[:, _j * 128:(_j + 1) * 128],
                                 xnow[:, _dc * 1024 + _th * 512:_dc * 1024 + _th * 512 + 512],
                                 start=(_dc == 0), stop=(_dc == 15)).then_inc(sem_fcd.h, 1)
                        if _l else
                        e.matmul(pbank[_j][:, :], _w[:, _j * 128:(_j + 1) * 128],
                                 xnow[:, _dc * 1024 + _th * 512:_dc * 1024 + _th * 512 + 512],
                                 start=(_dc == 0), stop=(_dc == 15))))
                fcd_t = sem_fcd.inc(1)
                r_wt.mark(fcd_t)
            P.wait("scalar", sem_fcd, fcd_t)
            for j in range(4):
                rslot = r_relu.slot()
                r_relu.guard(P, "scalar")
                rb = relub[:, rslot * 512:(rslot + 1) * 512]
                hs = hsq_slice(hg * 4 + j, th)
                P.add("scalar", lambda e, _j=j, _r=rb: e.activation(_r, pbank[_j][:, :], AF.Relu))
                P.add("scalar", lambda e, _r=rb, _h=hs: e.activation(
                    _h, _r, AF.Square).then_inc(sem_hv.h, 1))
                hv_t = sem_hv.inc(1)
                r_relu.mark(hv_t)
                r_fcp.mark(hv_t)

        P.wait("tensor", sem_hv, sem_hv.n)
        if th == 0:
            P.wait("tensor", sem_cc, AGPJ_T)
            P.wait("sync", sem_cc, AGPJ_T)
            P.wait("gpsimd", sem_cc, RS_T)
        for og in range(4):
            for ht in range(64):
                wslot = r_wt.slot()
                r_wt.guard(P, "sync")
                wtile = wtb[:, wslot * 512:(wslot + 1) * 512]
                _sw = sem_wtp[wslot]
                P.add("sync", lambda e, _og=og, _ht=ht, _w=wtile, _s=_sw: e.dma_start(
                    out=_w, in_=wqpj_full.ap()[_og * 512:(_og + 1) * 512, _ht * 128:(_ht + 1) * 128],
                    transpose=True).then_inc(_s.h, 16))
                wt_t = _sw.inc(16)
                P.wait("tensor", _sw, wt_t)
                if ht == 0 and (th > 0 or og >= 1):
                    P.wait("tensor", sem_g, r_pjp.marks[-1])
                for j in range(4):
                    last = (j == 3)
                    P.add("tensor", lambda e, _j=j, _ht=ht, _th=th, _w=wtile, _l=last: (
                        e.matmul(pbank[4 + _j][:, :], _w[:, _j * 128:(_j + 1) * 128],
                                 hsq_slice(_ht, _th),
                                 start=(_ht == 0), stop=(_ht == 63)).then_inc(sem_fcd.h, 1)
                        if _l else
                        e.matmul(pbank[4 + _j][:, :], _w[:, _j * 128:(_j + 1) * 128],
                                 hsq_slice(_ht, _th),
                                 start=(_ht == 0), stop=(_ht == 63))))
                fcd_t = sem_fcd.inc(1)
                r_wt.mark(fcd_t)
            for j in range(4):
                ot = og * 4 + j
                sxslot = r_sx.slot()
                r_sx.guard(P, "gpsimd")
                s_t = sb_s[:, sxslot * 512:(sxslot + 1) * 512]
                x_t = sb_x[:, sxslot * 512:(sxslot + 1) * 512]
                _ssx = sem_sxp[sxslot]
                P.add("gpsimd", lambda e, _ot=ot, _th=th, _s=s_t, _m=_ssx: e.dma_start(
                    out=_s, in_=sown_dram.ap()[_ot, :, _th * 512:(_th + 1) * 512]).then_inc(_m.h, 16))
                P.add("gpsimd", lambda e, _ot=ot, _th=th, _x=x_t, _m=_ssx: e.dma_start(
                    out=_x, in_=x_ownT[_ot * 128:(_ot + 1) * 128, _th * 512:(_th + 1) * 512]).then_inc(_m.h, 16))
                sx_t = _ssx.inc(32)

                P.wait("scalar", sem_fcd, fcd_t)
                gslot = r_gb.slot()
                r_gb.guard(P, "scalar")
                g_t = relub[:, (2 + gslot) * 512:(3 + gslot) * 512]
                P.add("scalar", lambda e, _j=j, _g=g_t: e.activation(
                    _g, pbank[4 + _j][:, :], AF.Silu).then_inc(sem_g.h, 1))
                g_tt = sem_g.inc(1)
                r_pjp.mark(g_tt)

                P.wait("vector", sem_g, g_tt)
                P.wait("vector", _ssx, sx_t)
                oslot = r_outb.slot()
                r_outb.guard(P, "vector")
                o_t = outb[:, oslot * 512:(oslot + 1) * 512]
                P.add("vector", lambda e, _s=s_t, _g=g_t, _o=o_t: e.tensor_mul(_o, _s, _g))
                P.add("vector", lambda e, _x=x_t, _o=o_t: e.tensor_add(
                    _o, _o, _x).then_inc(sem_go.h, 1))
                go_t = sem_go.inc(1)
                r_gb.mark(go_t)
                r_sx.mark(go_t)

                P.wait("gpsimd", sem_go, go_t)
                ost = dma("gpsimd", out[ot * 128:(ot + 1) * 128, th * 512:(th + 1) * 512], o_t,
                          sem_outp[oslot])
                r_outb.mark(ost, sem=sem_outp[oslot])

    P.wait("gpsimd", sem_outp[0], sem_outp[0].n)
    P.wait("gpsimd", sem_outp[1], sem_outp[1].n)
    if debug:
        dma("gpsimd", dbg["qh"], qh[:, :], sem_out)
        dma("gpsimd", dbg["kh"], kh[:, :], sem_out)
        dma("gpsimd", dbg["vt"], vt[:, :], sem_out)
        dma("gpsimd", dbg["xn"], xnow[:, :], sem_out)
        dma("gpsimd", dbg["mc"], mrow[:, 0:8], sem_out)
        P.wait("gpsimd", sem_ev, sem_ev.n)
        dma("gpsimd", dbg["ah"], ah[:, :], sem_out)
        P.wait("gpsimd", sem_cc, RS_T)
        dma("gpsimd", dbg["sown"], sown_dram.ap(), sem_out)
        dma("gpsimd", dbg["wqfc"], wqfc_full.ap(), sem_out)
        dma("gpsimd", dbg["wqpj"], wqpj_full.ap(), sem_out)
        dma("gpsimd", dbg["spart"], spart_dram.ap(), sem_out)
    P.wait("gpsimd", sem_out, sem_out.n)

    with nc.Block() as block:
        P.run(block)
    stack.close()
    mybir.codegen_inst_isa_subclasses(nc)
    return nc


_NC_CACHE = {}


def _get_nc(debug=DEBUG):
    key = ("v1", debug)
    if key not in _NC_CACHE:
        _NC_CACHE[key] = build_program(debug)
    return _NC_CACHE[key]


def kernel(L0_attn_x, L0_attn_q_w, L0_attn_kv_w, L0_attn_c_w, L0_ffn_w_fc, L0_ffn_w_proj,
           _debug=DEBUG, _return_raw=False):
    x = np.asarray(L0_attn_x, np.float32)
    q_w = np.asarray(L0_attn_q_w, np.float32)
    kv_w = np.asarray(L0_attn_kv_w, np.float32)
    c_w = np.asarray(L0_attn_c_w, np.float32)
    w_fc = np.asarray(L0_ffn_w_fc, np.float32)
    w_proj = np.asarray(L0_ffn_w_proj, np.float32)

    C2t, S2t = _rope_tables()
    perm = _perm128()
    trim = _tri_masks()
    in_maps = [_prep_core_inputs(c, x, q_w, kv_w, c_w, w_fc, w_proj, C2t, S2t, perm, trim)
               for c in range(8)]
    nc = _get_nc(_debug)
    res = run_bass_kernel_spmd(nc, in_maps, core_ids=list(range(8)))
    if _return_raw:
        return res
    outp = np.empty((2, 4096, 2048), np.float32)
    for c in range(8):
        b, r = c // 4, c % 4
        o = res.results[c]["out"]
        outp[b, r * 1024:(r + 1) * 1024, :] = o.T
    return outp


# version 04
# speedup vs baseline: 1.1263x; 1.1263x over previous
import contextlib
import math
import numpy as np

import concourse.bass as bass
import concourse.mybir as mybir
from concourse.bass_utils import run_bass_kernel_spmd
from concourse.dve_op import DveOp
import concourse.dve_ops as dve_ops
from concourse.dve_spec import (
    Spec, Src0, Src1, C0, C1, C2, C3, scan, AluOp, minn, maxx, sq, lower,
)
from concourse.dve_table_gen import dve_ver_for

F32 = mybir.dt.float32
F16 = mybir.dt.float16
BF16 = mybir.dt.bfloat16
AF = mybir.ActivationFunctionType
OP = mybir.AluOpType
AX = mybir.AxisListType

MAGIC = float(1.5 * 2.0 ** 23)
D = 2048
T = 4096
DK = 128
HID = 8192
QMAX = 15.0
SHRINKS = [0.4 + 0.04 * i for i in range(16)]
EPS = 1e-5
ISQ = 1.0 / math.sqrt(DK)

DEBUG = False

_REGISTERED = {}


def _register_op(name, spec, reference):
    if name in _REGISTERED:
        return _REGISTERED[name]
    ver = dve_ver_for("TRN2")
    sha = lower(spec, ver=ver, perf_en=False, name=name).sha(ver)
    op = DveOp(name, spec, subdim=False, uops_sha={ver: sha}, reference=reference)
    dve_ops.OPS.append(op)
    row = dve_ops._CUSTOM_DVE_ROW_BASE + dve_ops.OPS.index(op)
    assert row < 0x20, "custom-DVE opcode rows exhausted"
    dve_ops._SUB_OPCODE_FOR_NAME[name] = row
    dve_ops.CUSTOM_DVE_SPECS[name] = spec
    dve_ops.CUSTOM_DVE_REFS[name] = reference
    _REGISTERED[name] = op
    return op


def _ref_qdq_search(in0, in1, s0, s1, imm2):
    t1 = (in0 * np.asarray(s0, in0.dtype)).astype(in0.dtype)
    rc = np.clip(np.round(t1), -QMAX, QMAX).astype(in0.dtype)
    d = (t1 - rc).astype(in0.dtype)
    e = (d * d).astype(in0.dtype)
    flat = e.reshape(e.shape[0], -1)
    return np.cumsum(flat, axis=-1, dtype=in0.dtype).reshape(e.shape)


def _ref_finaldeq(in0, in1, s0, s1, imm2):
    return np.clip(np.round(in0), -QMAX, QMAX).astype(in0.dtype)


def _finaldeq_op():
    a = Src0 + C0
    b = minn(a, C1)
    c = maxx(b, C3)
    return _register_op("ATH_FINALDEQ", Spec(body=c - C0), _ref_finaldeq)


def _qdq_search_op():
    t1 = Src0 * C0
    a = t1 + C1
    b = minn(a, C2)
    c = maxx(b, C3)
    r = c - C1
    d = t1 - r
    e = sq(d)
    return _register_op("ATH_QDQ_SEARCH", Spec(body=scan(AluOp.ADD, e)), _ref_qdq_search)



def _rope_tables():
    pairs = np.arange(DK // 2, dtype=np.float32)
    freqs = (1.0 / (10000.0 ** (2.0 * pairs / DK))).astype(np.float32)
    pos = np.arange(T, dtype=np.float32)
    ang = pos[None, :] * freqs[:, None]
    cos = np.cos(ang).astype(np.float32)
    sin = np.sin(ang).astype(np.float32)
    C2t = np.empty((128, T), np.float32)
    S2t = np.empty((128, T), np.float32)
    for p in range(128):
        g, slot = p // 32, p % 32
        i = g * 16 + (slot % 16)
        C2t[p] = cos[i]
        S2t[p] = (-sin[i]) if slot < 16 else sin[i]
    return C2t, S2t


def _perm128():
    d_of_p = np.empty(128, np.int64)
    for p in range(128):
        g, slot = p // 32, p % 32
        i = g * 16 + (slot % 16)
        d_of_p[p] = 2 * i + (0 if slot < 16 else 1)
    return d_of_p


def _tri_masks():
    import ml_dtypes
    out = np.zeros((4, 128, 512), ml_dtypes.bfloat16)
    for kt in range(4):
        for k in range(128):
            out[kt, k, kt * 128 + k:] = 1.0
    return out


def _prep_core_inputs(c, x, q_w, kv_w, c_w, w_fc, w_proj, C2t, S2t, perm, tri):
    b, r = c // 4, c % 4
    d = {}
    d["x_fullT"] = np.ascontiguousarray(x[b].T)
    d["x_ownT"] = np.ascontiguousarray(x[b, r * 1024:(r + 1) * 1024].T)
    rowsel = np.concatenate([512 * r + hh * 128 + perm for hh in range(4)])
    d["qwT"] = np.ascontiguousarray(q_w[rowsel, :].T.astype(np.float16))
    d["kwT"] = np.ascontiguousarray(kv_w[r * 128 + perm, :].T.astype(np.float16))
    d["vwT"] = np.ascontiguousarray(kv_w[512 + r * 128 + np.arange(128), :].T.astype(np.float16))
    d["cwT"] = np.ascontiguousarray(c_w[:, 512 * r:512 * (r + 1)].T.astype(np.float16))
    d["ropeC"] = C2t
    d["ropeS"] = S2t
    d["tri"] = tri
    d["wfc_sh"] = np.ascontiguousarray(w_fc[c * 1024:(c + 1) * 1024, :])
    d["wpj_sh"] = np.ascontiguousarray(w_proj[c * 256:(c + 1) * 256, :])
    shr = np.asarray(SHRINKS, np.float32)
    pat = np.empty((2, 128, 512), np.float32)
    pat[0] = np.tile(-(shr * shr), 32)[None, :]
    pat[1] = np.tile(np.arange(16, dtype=np.float32), 32)[None, :]
    d["qpat"] = pat
    return d



class Sem:
    def __init__(self, nc, stack, name):
        self.h = stack.enter_context(nc.semaphore(name))
        self.n = 0

    def inc(self, k=1):
        self.n += k
        return self.n


class Ring:

    def __init__(self, depth, sem):
        self.depth = depth
        self.sem = sem
        self.marks = []

    def guard(self, P, eng):
        i = len(self.marks)
        if i >= self.depth:
            m = self.marks[i - self.depth]
            if isinstance(m, tuple):
                P.wait(eng, m[0], m[1])
            else:
                P.wait(eng, self.sem, m)

    def mark(self, val=None, sem=None):
        if sem is not None:
            self.marks.append((sem, sem.n if val is None else val))
        else:
            self.marks.append(self.sem.n if val is None else val)

    def slot(self):
        return len(self.marks) % self.depth


class Prog:
    ENGINES = ("sync", "vector", "scalar", "tensor", "gpsimd")

    def __init__(self):
        self.q = {e: [] for e in self.ENGINES}

    def add(self, eng, fn):
        self.q[eng].append(fn)

    def wait(self, eng, sem, val=None):
        v = sem.n if val is None else val
        if v > 0:
            self.q[eng].append(lambda e, _s=sem.h, _v=v: e.wait_ge(_s, _v))

    def run(self, block):
        for eng in self.ENGINES:
            fns = self.q[eng]

            def body(e, _fns=fns):
                for f in _fns:
                    f(e)
            getattr(block, eng)(body)


SH_SWAP16 = [(j + 16) % 32 for j in range(32)]


def build_program(debug=DEBUG):
    _qdq_search_op()
    _finaldeq_op()
    QDQ_OP = _REGISTERED["ATH_QDQ_SEARCH"]
    FDQ_OP = _REGISTERED["ATH_FINALDEQ"]
    nc = bass.Bass("TRN2", target_bir_lowering=False, debug=False,
                   detect_race_conditions=False)

    x_fullT = nc.dram_tensor("x_fullT", [D, T], F32, kind="ExternalInput").ap()
    x_ownT = nc.dram_tensor("x_ownT", [D, 1024], F32, kind="ExternalInput").ap()
    qwT = nc.dram_tensor("qwT", [D, 512], F16, kind="ExternalInput").ap()
    kwT = nc.dram_tensor("kwT", [D, 128], F16, kind="ExternalInput").ap()
    vwT = nc.dram_tensor("vwT", [D, 128], F16, kind="ExternalInput").ap()
    cwT = nc.dram_tensor("cwT", [512, D], F16, kind="ExternalInput").ap()
    ropeC = nc.dram_tensor("ropeC", [128, T], F32, kind="ExternalInput").ap()
    ropeS = nc.dram_tensor("ropeS", [128, T], F32, kind="ExternalInput").ap()
    tri = nc.dram_tensor("tri", [4, 128, 512], BF16, kind="ExternalInput").ap()
    wfc_sh = nc.dram_tensor("wfc_sh", [1024, D], F32, kind="ExternalInput").ap()
    wpj_sh = nc.dram_tensor("wpj_sh", [256, HID], F32, kind="ExternalInput").ap()
    qpat = nc.dram_tensor("qpat", [2, 128, 512], F32, kind="ExternalInput").ap()

    out = nc.dram_tensor("out", [D, 1024], F32, kind="ExternalOutput").ap()

    spart_dram = nc.dram_tensor("spart_dram", [4, 16, 128, 1024], F16)
    sown_dram = nc.dram_tensor("sown_dram", [16, 128, 1024], F16)
    wqfc_b = nc.dram_tensor("wqfc_b", [1024, D], F16)
    wqfc_full = nc.dram_tensor("wqfc_full", [HID, D], F16)
    wqpj_bA = nc.dram_tensor("wqpj_bA", [256, HID // 2], F16)
    wqpj_bB = nc.dram_tensor("wqpj_bB", [256, HID // 2], F16)
    wqpj_fullA = nc.dram_tensor("wqpj_fullA", [D, HID // 2], F16)
    wqpj_fullB = nc.dram_tensor("wqpj_fullB", [D, HID // 2], F16)
    wqfcT = nc.dram_tensor("wqfcT", [16, 128, HID], F16)
    wqpjT = nc.dram_tensor("wqpjT", [64, 128, D], F16)

    dbg = {}
    if debug:
        dbg["qh"] = nc.dram_tensor("dbg_qh", [128, 4 * T], F16, kind="ExternalOutput").ap()
        dbg["kh"] = nc.dram_tensor("dbg_kh", [128, T], F16, kind="ExternalOutput").ap()
        dbg["vt"] = nc.dram_tensor("dbg_vt", [128, T], BF16, kind="ExternalOutput").ap()
        dbg["xn"] = nc.dram_tensor("dbg_xn", [128, 16 * 1024], F16, kind="ExternalOutput").ap()
        dbg["mc"] = nc.dram_tensor("dbg_mc", [1, 8], F32, kind="ExternalOutput").ap()
        dbg["ah"] = nc.dram_tensor("dbg_ah", [128, 4 * T], F16, kind="ExternalOutput").ap()
        dbg["sown"] = nc.dram_tensor("dbg_sown", [16, 128, 1024], F16, kind="ExternalOutput").ap()
        dbg["wqfc"] = nc.dram_tensor("dbg_wqfc", [HID, D], F16, kind="ExternalOutput").ap()
        dbg["wqpj"] = nc.dram_tensor("dbg_wqpj", [D, HID], F16, kind="ExternalOutput").ap()
        dbg["spart"] = nc.dram_tensor("dbg_spart", [4, 16, 128, 1024], F16, kind="ExternalOutput").ap()

    stack = contextlib.ExitStack()
    sb = lambda name, shape, dt=F16: stack.enter_context(nc.sbuf_tensor(name, shape, dt))
    ps = lambda name, shape, dt=F32: stack.enter_context(nc.psum_tensor(name, shape, dt))
    S = lambda name: Sem(nc, stack, name)

    TT = 256
    NTT = T // TT

    qh = sb("qh", [128, 4 * T])
    kh = sb("kh", [128, T])
    vt = sb("vt", [128, T], BF16)
    xnow = sb("xnow", [128, 16 * 1024])
    arenaA = sb("arenaA", [128, 8192], F32)
    big1 = sb("big1", [128, 16384])
    nsqb = sb("nsqb", [128, 2 * TT])
    ropc = sb("ropc", [128, 2 * TT], F32)
    rops = sb("rops", [128, 2 * TT], F32)
    qwTs = sb("qwTs", [128, 16 * 512])
    kwTs = sb("kwTs", [128, 16 * 128])
    vwTs = sb("vwTs", [128, 16 * 128])
    srow = sb("srow", [1, 5 * TT], F32)
    mrow = sb("mrow", [1, 16], F32)
    nrow = sb("nrow", [1, 128], F32)
    mcore = sb("mcore", [128, 1], F32)
    rtmp1 = sb("rtmp1", [128, TT], F32)
    rtmp2 = sb("rtmp2", [128, TT], F32)
    rtmp3 = sb("rtmp3", [128, TT], F32)
    ones16 = sb("ones16", [128, 1])
    ones32 = sb("ones32", [128, 1], F32)
    onesrow32 = sb("onesrow32", [1, 128], F32)
    ones_bf = sb("ones_bf", [128, 1], BF16)
    trims = sb("trims", [128, 4 * 512], BF16)
    probsb = sb("probsb", [128, 3 * 512], BF16)
    sparth = sb("sparth", [128, 2 * 512])
    rdrow = sb("rdrow", [1, 512], F32)
    qsm = sb("qsm", [128, 8 * 32], F32)
    gatall = sb("gatall", [128, 512], F32)
    errsB = sb("errsB", [128, 512], F32)
    maskb = sb("maskb", [128, 512], mybir.dt.uint8)
    qpats = sb("qpats", [128, 2 * 512], F32)
    c3col = sb("c3col", [128, 1], F32)
    wqh = sb("wqh", [128, 2048])
    xf = arenaA
    xsq_v = big1[:, 0:16 * TT]
    xn_v = big1[:, 16 * TT:3 * 16 * TT]
    ah = big1

    pbank = [ps(f"pb{i}", [128, 512]) for i in range(8)]
    ps_stat = [pbank[0][0:1, :], pbank[1][0:1, :]]
    ps_bc = [pbank[2], pbank[3]]
    ps_proj = [pbank[4][:, 0:TT], pbank[5][:, 0:TT]]
    ps_nrm = [pbank[6][0:1, 0:TT], pbank[7][0:1, 0:TT]]

    sem_in = S("sem_in")
    sem_xp = [S("sem_x0"), S("sem_x1")]
    sem_rop = [S("sem_ro0"), S("sem_ro1")]
    sem_sq = S("sem_sq")
    sem_mmst = S("sem_mmst")
    sem_st1 = S("sem_st1")
    sem_sd = S("sem_sd")
    sem_rst = S("sem_rst")
    sem_bc = S("sem_bc")
    sem_xn = S("sem_xn")
    sem_qp = S("sem_qp")
    sem_rp = S("sem_rp")
    sem_nrm = S("sem_nrm")
    sem_nev = S("sem_nev")
    sem_mc = S("sem_mc")
    sem_out = S("sem_out")
    sem_qk = S("sem_qk")
    sem_pr = S("sem_pr")
    sem_pm = S("sem_pm")
    sem_pv = S("sem_pv")
    sem_rd = S("sem_rd")
    sem_bc2 = S("sem_bc2")
    sem_pt = S("sem_pt")
    sem_ev = S("sem_ev")
    sem_cp = S("sem_cp")
    sem_ce = S("sem_ce")
    sem_sp = S("sem_sp")
    sem_cw = S("sem_cw")
    sem_cc = S("sem_cc")
    sem_wtp = [S("sem_wt0"), S("sem_wt1")]
    sem_twp = [S("sem_tw0"), S("sem_tw1")]
    sem_tsp = [S("sem_ts0"), S("sem_ts1")]
    sem_pwp = [S("sem_pw0"), S("sem_pw1")]
    sem_pjt = S("sem_pjt")
    sem_sxp = [S("sem_sx0"), S("sem_sx1")]
    sem_outp = [S("sem_out0"), S("sem_out1")]
    sem_spp = [S("sem_spp0"), S("sem_spp1")]

    P = Prog()

    def dma(eng, dst, src, sem):
        P.add(eng, lambda e, _d=dst, _s=src: e.dma_start(out=_d, in_=_s).then_inc(sem.h, 16))
        return sem.inc(16)

    r_xf = Ring(2, sem_xn)
    r_rope = Ring(2, sem_rp)
    r_xsq = Ring(1, sem_mmst)
    r_xn = Ring(2, sem_qp)
    r_nsqb = Ring(2, sem_nrm)
    r_stat = Ring(2, sem_st1)
    r_bc = Ring(2, sem_xn)
    r_proj = Ring(2, sem_rp)
    r_nrm = Ring(2, sem_nev)
    r_srow = Ring(1, sem_bc)

    dma("sync", qwTs.ap().rearrange("p (c q) -> p c q", c=16),
        qwT.rearrange("(c p) q -> p c q", p=128), sem_in)
    dma("sync", kwTs.ap().rearrange("p (c q) -> p c q", c=16),
        kwT.rearrange("(c p) q -> p c q", p=128), sem_in)
    dma("sync", vwTs.ap().rearrange("p (c q) -> p c q", c=16),
        vwT.rearrange("(c p) q -> p c q", p=128), sem_in)
    IN_W = sem_in.n

    P.add("vector", lambda e: e.memset(ones16[:, :], 1.0))
    P.add("vector", lambda e: e.memset(ones32[:, :], 1.0))
    P.add("vector", lambda e: e.memset(onesrow32[:, :], 1.0))
    P.add("vector", lambda e: e.memset(ones_bf[:, :], 1.0))
    P.add("vector", lambda e: e.memset(mrow[:, :], 0.0).then_inc(sem_st1.h, 1))
    CONSTS_RDY = sem_st1.inc(1)
    P.wait("tensor", sem_st1, CONSTS_RDY)

    def ln_tile(i, pass_own):
        par = i % 2
        xfi = xf[:, par * 16 * TT:(par + 1) * 16 * TT].rearrange("p (c t) -> p c t", c=16)
        xsqi = xsq_v.rearrange("p (c t) -> p c t", c=16)
        xni = xn_v[:, par * 16 * TT:(par + 1) * 16 * TT].rearrange("p (c t) -> p c t", c=16)
        tok0 = i * TT
        xin = x_ownT if pass_own else x_fullT

        r_xf.guard(P, "sync")
        xld = dma("sync", xfi, xin.rearrange("(c p) t -> p c t", p=128)[:, :, tok0:tok0 + TT], sem_xp[par])
        if not pass_own:
            r_rope.guard(P, "sync")
            dma("sync", ropc[:, par * TT:(par + 1) * TT], ropeC[:, tok0:tok0 + TT], sem_rop[par])
            rld = dma("sync", rops[:, par * TT:(par + 1) * TT], ropeS[:, tok0:tok0 + TT], sem_rop[par])

        P.wait("scalar", sem_xp[par], xld)
        r_xsq.guard(P, "scalar")
        for c in range(16):
            last = c == 15
            P.add("scalar", lambda e, _c=c, _l=last: (
                e.activation(xsqi[:, _c, :], xfi[:, _c, :], AF.Square).then_inc(sem_sq.h, 1)
                if _l else e.activation(xsqi[:, _c, :], xfi[:, _c, :], AF.Square)))
        sqd = sem_sq.inc(1)

        P.wait("tensor", sem_xp[par], xld)
        r_stat.guard(P, "tensor")
        pst = ps_stat[par]
        for c in range(16):
            P.add("tensor", lambda e, _c=c: e.matmul(
                pst[:, 0:TT], ones32[:, :], xfi[:, _c, :], start=(_c == 0), stop=(_c == 15)))
        P.wait("tensor", sem_sq, sqd)
        for c in range(16):
            last = c == 15
            P.add("tensor", lambda e, _c=c, _l=last: (
                e.matmul(pst[:, TT:2 * TT], ones16[:, :], xsqi[:, _c, :],
                         start=(_c == 0), stop=(_c == 15)).then_inc(sem_mmst.h, 1)
                if _l else
                e.matmul(pst[:, TT:2 * TT], ones16[:, :], xsqi[:, _c, :],
                         start=(_c == 0), stop=(_c == 15))))
        mmst = sem_mmst.inc(1)
        r_xsq.mark()
        r_xf_partial_done = mmst

        P.wait("vector", sem_mmst, mmst)
        r_srow.guard(P, "vector")
        P.add("vector", lambda e: e.tensor_scalar(
            out=srow[:, 0:TT], in0=pst[:, 0:TT], scalar1=1.0 / D, scalar2=None, op0=OP.mult))
        P.add("vector", lambda e: e.tensor_scalar(
            out=srow[:, TT:2 * TT], in0=pst[:, TT:2 * TT], scalar1=1.0 / D, scalar2=EPS,
            op0=OP.mult, op1=OP.add))
        P.add("vector", lambda e: e.tensor_mul(srow[:, 2 * TT:3 * TT], srow[:, 0:TT], srow[:, 0:TT]))
        P.add("vector", lambda e: e.tensor_sub(
            srow[:, TT:2 * TT], srow[:, TT:2 * TT], srow[:, 2 * TT:3 * TT]).then_inc(sem_st1.h, 1))
        st1 = sem_st1.inc(1)
        r_stat.mark()

        P.wait("scalar", sem_st1, st1)
        P.add("scalar", lambda e: e.activation(
            srow[:, 2 * TT:3 * TT], srow[:, TT:2 * TT], AF.Sqrt).then_inc(sem_sd.h, 1))
        sdd = sem_sd.inc(1)

        P.wait("vector", sem_sd, sdd)
        P.add("vector", lambda e: e.reciprocal(srow[:, 3 * TT:4 * TT], srow[:, 2 * TT:3 * TT]))
        P.add("vector", lambda e: e.tensor_mul(
            srow[:, 4 * TT:5 * TT], srow[:, 0:TT], srow[:, 3 * TT:4 * TT]).then_inc(sem_rst.h, 1))
        rst = sem_rst.inc(1)

        P.wait("tensor", sem_rst, rst)
        r_bc.guard(P, "tensor")
        pbc = ps_bc[par]
        P.add("tensor", lambda e: e.matmul(
            pbc[:, 0:TT], onesrow32[:, :], srow[:, 3 * TT:4 * TT], start=True, stop=True))
        P.add("tensor", lambda e: e.matmul(
            pbc[:, TT:2 * TT], onesrow32[:, :], srow[:, 4 * TT:5 * TT],
            start=True, stop=True).then_inc(sem_bc.h, 1))
        bcd = sem_bc.inc(1)
        r_srow.mark()

        P.wait("vector", sem_bc, bcd)
        r_xn.guard(P, "vector")
        for c in range(16):
            last = c == 15
            P.add("vector", lambda e, _c=c: e.tensor_mul(rtmp3[:, :], xfi[:, _c, :], pbc[:, 0:TT]))
            P.add("vector", lambda e, _c=c, _l=last: (
                e.tensor_sub(xni[:, _c, :], rtmp3[:, :], pbc[:, TT:2 * TT]).then_inc(sem_xn.h, 1)
                if _l else e.tensor_sub(xni[:, _c, :], rtmp3[:, :], pbc[:, TT:2 * TT])))
        if pass_own:
            for c in range(16):
                P.add("vector", lambda e, _c=c: e.tensor_copy(
                    out=xnow[:, _c * 1024 + tok0:_c * 1024 + tok0 + TT], in_=xni[:, _c, :]))
        P.add("vector", lambda e: e.engine_nop().then_inc(sem_xn.h, 1))
        xnd = sem_xn.inc(2)
        r_xf.mark(xnd)
        r_bc.mark(xnd)

        if pass_own:
            r_xn.mark(sem_qp.n)
            r_rope.mark(sem_rp.n)
            return

        if i == 0:
            P.wait("tensor", sem_in, IN_W)
        P.wait("tensor", sem_xn, xnd)
        for u in range(5):
            pp = ps_proj[r_proj.slot()]
            r_proj.guard(P, "tensor")
            for c in range(16):
                if u < 4:
                    lhs = qwTs[:, c * 512 + u * 128: c * 512 + u * 128 + 128]
                else:
                    lhs = kwTs[:, c * 128:(c + 1) * 128]
                last = c == 15
                P.add("tensor", lambda e, _l=lhs, _c=c, _last=last, _pp=pp: (
                    e.matmul(_pp[:, :], _l, xni[:, _c, :],
                             start=(_c == 0), stop=(_c == 15)).then_inc(sem_qp.h, 1)
                    if _last else
                    e.matmul(_pp[:, :], _l, xni[:, _c, :], start=(_c == 0), stop=(_c == 15))))
            qpd = sem_qp.inc(1)

            P.wait("vector", sem_qp, qpd)
            if u == 0:
                P.wait("vector", sem_rop[par], rld)
            dst = qh[:, u * T + tok0: u * T + tok0 + TT] if u < 4 else kh[:, tok0:tok0 + TT]
            P.add("vector", lambda e, _pp=pp: e.tensor_copy(out=rtmp3[:, :], in_=_pp[:, :]))
            P.add("vector", lambda e: e.stream_shuffle(out=rtmp1[:, :], in_=rtmp3[:, :], mask=SH_SWAP16))
            P.add("vector", lambda e: e.tensor_mul(
                rtmp2[:, :], rtmp3[:, :], ropc[:, par * TT:(par + 1) * TT]))
            P.add("vector", lambda e: e.tensor_mul(
                rtmp1[:, :], rtmp1[:, :], rops[:, par * TT:(par + 1) * TT]))
            P.add("vector", lambda e, _d=dst: e.tensor_add(
                _d, rtmp2[:, :], rtmp1[:, :]).then_inc(sem_rp.h, 1))
            rpd = sem_rp.inc(1)
            r_proj.mark(rpd)
            if u == 4:
                r_rope.mark(rpd)

            P.wait("scalar", sem_rp, rpd)
            nslot = r_nsqb.slot()
            r_nsqb.guard(P, "scalar")
            nb = nsqb[:, nslot * TT:(nslot + 1) * TT]
            P.add("scalar", lambda e, _d=dst, _n=nb: e.activation(
                _n, _d, AF.Square).then_inc(sem_sq.h, 1))
            nsq = sem_sq.inc(1)

            P.wait("tensor", sem_sq, nsq)
            pn = ps_nrm[r_nrm.slot()]
            r_nrm.guard(P, "tensor")
            P.add("tensor", lambda e, _n=nb, _p=pn: e.matmul(
                _p[:, :], ones16[:, :], _n, start=True, stop=True).then_inc(sem_nrm.h, 1))
            nmd = sem_nrm.inc(1)
            r_nsqb.mark(nmd)

            P.wait("vector", sem_nrm, nmd)
            idx = (i * 4 + u) if u < 4 else (64 + i)
            P.add("vector", lambda e, _p=pn, _ix=idx: e.reduce_max(
                nrow[:, _ix:_ix + 1], _p[:, :], axis=AX.X).then_inc(sem_nev.h, 1))
            nev = sem_nev.inc(1)
            r_nrm.mark(nev)

        for v in range(TT // 128):
            pp = ps_proj[r_proj.slot()]
            r_proj.guard(P, "tensor")
            for c in range(16):
                last = c == 15
                P.add("tensor", lambda e, _c=c, _v=v, _last=last, _pp=pp: (
                    e.matmul(_pp[:, 0:128], xni[:, _c, _v * 128:(_v + 1) * 128],
                             vwTs[:, _c * 128:(_c + 1) * 128],
                             start=(_c == 0), stop=(_c == 15)).then_inc(sem_qp.h, 1)
                    if _last else
                    e.matmul(_pp[:, 0:128], xni[:, _c, _v * 128:(_v + 1) * 128],
                             vwTs[:, _c * 128:(_c + 1) * 128],
                             start=(_c == 0), stop=(_c == 15))))
            qpd = sem_qp.inc(1)
            P.wait("scalar", sem_qp, qpd)
            ktile = (tok0 + v * 128) // 128
            P.add("scalar", lambda e, _k=ktile, _pp=pp: e.activation(
                vt[:, _k * 128:(_k + 1) * 128], _pp[:, 0:128], AF.Copy).then_inc(sem_rp.h, 1))
            rpd = sem_rp.inc(1)
            r_proj.mark(rpd)

        r_xn.mark(qpd)

    for i in range(NTT):
        ln_tile(i, pass_own=False)

    P.wait("vector", sem_nev, sem_nev.n)
    P.add("vector", lambda e: e.tensor_copy(out=mrow[:, 8:10], in_=mrow[:, 8:10]))
    P.add("vector", lambda e: e.reduce_max(mrow[:, 0:1], nrow[:, 0:64], axis=AX.X))
    P.add("vector", lambda e: e.reduce_max(mrow[:, 1:2], nrow[:, 64:80], axis=AX.X))
    P.add("vector", lambda e: e.tensor_copy(out=mrow[:, 8:10], in_=mrow[:, 8:10]))
    P.add("vector", lambda e: e.tensor_mul(
        mrow[:, 2:3], mrow[:, 0:1], mrow[:, 1:2]).then_inc(sem_st1.h, 1))
    st1 = sem_st1.inc(1)
    P.wait("scalar", sem_st1, st1)
    P.add("scalar", lambda e: e.activation(
        mrow[:, 3:4], mrow[:, 2:3], AF.Sqrt).then_inc(sem_sd.h, 1))
    sdd = sem_sd.inc(1)
    P.wait("tensor", sem_sd, sdd)
    P.add("tensor", lambda e: e.matmul(
        ps_bc[0][:, 0:1], onesrow32[:, :], mrow[:, 3:4], start=True, stop=True).then_inc(sem_nrm.h, 1))
    nmd = sem_nrm.inc(1)
    P.wait("vector", sem_nrm, nmd)
    P.add("vector", lambda e: e.tensor_scalar(
        out=mcore[:, :], in0=ps_bc[0][:, 0:1], scalar1=-ISQ, scalar2=None,
        op0=OP.mult).then_inc(sem_mc.h, 1))
    sem_mc.inc(1)

    P.wait("tensor", sem_mc, 1)
    P.wait("sync", sem_qp, sem_qp.n)
    dma("sync", qwTs.ap().rearrange("p (c o) -> p c o", c=4),
        cwT.rearrange("(c p) o -> p c o", p=128), sem_cw)
    CW_T = sem_cw.n
    dma("sync", trims.ap().rearrange("p (k q) -> p k q", k=4),
        tri.transpose([1, 0, 2]), sem_cw)
    TRI_T = sem_cw.n
    for i in range(1024 // TT):
        ln_tile(i, pass_own=True)
    cws = qwTs

    for _eng in ("tensor", "vector", "scalar"):
        for _sem in (sem_st1, sem_rst, sem_xn, sem_rp, sem_nev, sem_mc, sem_sq,
                     sem_mmst, sem_bc, sem_qp, sem_nrm, sem_in, sem_xp[0], sem_xp[1],
                     sem_rop[0], sem_rop[1], sem_sd):
            P.wait(_eng, _sem, _sem.n)
        P.wait(_eng, sem_cw, TRI_T)

    r_lg = Ring(2, sem_pr)
    pvs_t = []
    gexp = [0]
    r_pv = Ring(2, sem_pt)
    r_sums = Ring(2, sem_rd)
    r_rdx = Ring(2, sem_ev)
    pvtmp = ropc
    r_pvt = Ring(1, sem_ev)
    r_rdr = Ring(1, sem_bc2)
    r_sph = Ring(2, sem_sp)

    sem_qw = S("sem_qw")
    sem_qu = S("sem_qu")
    sem_qv = S("sem_qv")
    sem_qs = S("sem_qs")
    sem_qp2 = S("sem_qp2")

    dma("sync", qpats.ap().rearrange("p (k q) -> p k q", k=2), qpat.transpose([1, 0, 2]), sem_qp2)
    P.wait("vector", sem_qp2, sem_qp2.n)
    P.add("vector", lambda e: e.memset(c3col[:, :], MAGIC - 15.0))

    wv = arenaA[:, 0:2048]
    uv = arenaA[:, 2048:4096]
    prefs = [arenaA[:, 4096:6144], arenaA[:, 6144:8192]]
    AM, BA, IB, SI, SS, IV, SB2 = [qsm[:, i * 32:(i + 1) * 32] for i in range(7)]
    junk = qsm[:, 7 * 32:8 * 32]

    P.wait("sync", sem_xn, sem_xn.n)
    r_qw = Ring(1, sem_qu)
    r_wqh = Ring(1, sem_qs)

    def g3(t, n=64):
        return t.rearrange("p (b k) -> p b k", k=n)

    def emit_qdq_unit(qu):
            if qu < 8:
                src = wfc_sh[qu * 128:(qu + 1) * 128, :]
            else:
                t8, ch = (qu - 8) // 4, (qu - 8) % 4
                src = wpj_sh[t8 * 128:(t8 + 1) * 128, ch * 2048:(ch + 1) * 2048]
            r_qw.guard(P, "sync")
            qw_t = dma("sync", wv, src, sem_qw)

            P.wait("vector", sem_qw, qw_t)
            P.add("vector", lambda e: e.reduce_max(AM, g3(wv), axis=AX.X, apply_absolute_value=True))
            P.add("vector", lambda e: e.tensor_copy(out=junk, in_=junk))
            P.add("vector", lambda e: e.tensor_scalar(
                out=BA, in0=AM, scalar1=1e-8, scalar2=1.0 / QMAX, op0=OP.max, op1=OP.mult))
            P.add("vector", lambda e: e.tensor_copy(out=junk, in_=junk))
            P.add("vector", lambda e: e.reciprocal(IB, BA))
            P.add("vector", lambda e: e.tensor_copy(out=junk, in_=junk))
            P.add("vector", lambda e: e.tensor_mul(
                g3(uv), g3(wv), IB.unsqueeze(2).broadcast_to([128, 32, 64])).then_inc(sem_qu.h, 1))
            r_qw.mark(sem_qu.inc(1))
            P.add("vector", lambda e: e.tensor_copy(out=junk, in_=junk))

            for si in range(16):
                pf = prefs[si % 2]
                P.add("vector", lambda e, _si=si, _pf=pf: e._custom_dve(
                    QDQ_OP, out=_pf, in0=uv, in1=c3col[:, :],
                    s0=float(1.0 / SHRINKS[_si]), s1=MAGIC, imm2=MAGIC + 15.0))
                if si >= 1:
                    pprev = prefs[(si - 1) % 2]
                    P.add("vector", lambda e, _si=si, _p=pprev: e.tensor_copy(
                        out=gatall[:, (_si - 1) * 32:_si * 32], in_=g3(_p)[:, :, 63]))
            P.add("vector", lambda e: e.tensor_copy(out=junk, in_=junk))
            P.add("vector", lambda e: e.tensor_copy(
                out=gatall[:, 15 * 32:16 * 32], in_=g3(prefs[1])[:, :, 63]))

            gat3 = gatall.ap().rearrange("p (s b) -> p s b", s=16)
            eview = errsB.ap().rearrange("p (b s) -> p b s", s=16).transpose([0, 2, 1])
            VG = lambda: P.add("vector", lambda e: e.tensor_copy(out=junk, in_=junk))
            VG()
            P.add("vector", lambda e: e.tensor_sub(eview[:, :, 1:32], gat3[:, :, 1:32], gat3[:, :, 0:31]))
            P.add("vector", lambda e: e.tensor_copy(out=eview[:, :, 0:1], in_=gat3[:, :, 0:1]))
            VG()
            P.add("vector", lambda e: e.tensor_mul(errsB[:, :], errsB[:, :], qpats[:, 0:512]))
            eb3 = errsB.ap().rearrange("p (b s) -> p b s", s=16)
            VG()
            P.add("vector", lambda e: e.reduce_max(SI, eb3, axis=AX.X))
            VG()
            P.add("vector", lambda e: e.tensor_tensor(
                maskb.ap().rearrange("p (b s) -> p b s", s=16), eb3,
                SI.unsqueeze(2).broadcast_to([128, 32, 16]), OP.is_ge))
            P.add("vector", lambda e: e.memset(gatall[:, :], 1e9))
            VG()
            P.add("vector", lambda e: e.copy_predicated(gatall[:, :], maskb[:, :], qpats[:, 512:1024]))
            VG()
            P.add("vector", lambda e: e.tensor_reduce(
                SI, gatall.ap().rearrange("p (b s) -> p b s", s=16), axis=AX.X, op=OP.min))
            VG()
            P.add("vector", lambda e: e.tensor_scalar(
                out=SS, in0=SI, scalar1=0.04, scalar2=0.4, op0=OP.mult, op1=OP.add))
            P.add("vector", lambda e: e.tensor_copy(out=junk, in_=junk))
            P.add("vector", lambda e: e.reciprocal(IV, SS))
            P.add("vector", lambda e: e.tensor_mul(SB2, SS, BA))
            r_wqh.guard(P, "vector")
            wqv = wqh[:, 0:2048]
            P.add("vector", lambda e: e.tensor_mul(
                g3(prefs[1]), g3(uv), IV.unsqueeze(2).broadcast_to([128, 32, 64])))
            P.add("vector", lambda e: e.tensor_copy(out=junk, in_=junk))
            P.add("vector", lambda e: e._custom_dve(
                FDQ_OP, out=prefs[0], in0=prefs[1], in1=c3col[:, :],
                s0=MAGIC, s1=MAGIC + 15.0))
            P.add("vector", lambda e: e.tensor_copy(out=junk, in_=junk))
            P.add("vector", lambda e: e.tensor_mul(
                g3(wqv), g3(prefs[0]),
                SB2.unsqueeze(2).broadcast_to([128, 32, 64])).then_inc(sem_qv.h, 1))
            qv_t = sem_qv.inc(1)

            P.wait("gpsimd", sem_qv, qv_t)
            if qu < 8:
                dst = wqfc_b.ap()[qu * 128:(qu + 1) * 128, :]
            else:
                t8, ch = (qu - 8) // 4, (qu - 8) % 4
                wb = wqpj_bA if ch < 2 else wqpj_bB
                dst = wb.ap()[t8 * 128:(t8 + 1) * 128, (ch % 2) * 2048:(ch % 2 + 1) * 2048]
            qs_t = dma("gpsimd", dst, wqv, sem_qs)
            r_wqh.mark(qs_t)

            if qu == 7:
                P.wait("gpsimd", sem_qs, sem_qs.n)
                P.add("gpsimd", lambda e: e.collective_compute(
                    "AllGather", OP.bypass, replica_groups=[list(range(8))],
                    ins=[wqfc_b.ap().opt()], outs=[wqfc_full.ap().opt()],
                ).then_inc(sem_cc.h, 1))
                agt["fc"] = sem_cc.inc(1)
            if qu == 13:
                P.wait("gpsimd", sem_qs, sem_qs.n)
                P.add("gpsimd", lambda e: e.collective_compute(
                    "AllGather", OP.bypass, replica_groups=[list(range(8))],
                    ins=[wqpj_bA.ap().opt()], outs=[wqpj_fullA.ap().opt()],
                ).then_inc(sem_cc.h, 1))
                agt["pjA"] = sem_cc.inc(1)
            if qu == 15:
                P.wait("gpsimd", sem_qs, sem_qs.n)
                P.add("gpsimd", lambda e: e.collective_compute(
                    "AllGather", OP.bypass, replica_groups=[list(range(8))],
                    ins=[wqpj_bB.ap().opt()], outs=[wqpj_fullB.ap().opt()],
                ).then_inc(sem_cc.h, 1))
                agt["pjB"] = sem_cc.inc(1)

    agt = {}
    qdq_next = [0]
    tw_ring = Ring(2, sem_tsp[0])
    tw_cnt = [0]

    def emit_fc_transpose(dc):
        for q4 in range(4):
            slot = tw_cnt[0] % 2
            buf = (kwTs if slot == 0 else vwTs)[:, 0:2048]
            tw_ring.guard(P, "sync")
            if tw_cnt[0] == 0:
                P.wait("sync", sem_cc, agt["fc"])
            src = wqfc_full.ap()[q4 * 2048:(q4 + 1) * 2048, dc * 128:(dc + 1) * 128]
            P.add("sync", lambda e, _s=src, _b=buf, _m=sem_twp[slot]: e.dma_start(
                out=_b, in_=_s, transpose=True).then_inc(_m.h, 16))
            tw_t = sem_twp[slot].inc(16)
            P.wait("gpsimd", sem_twp[slot], tw_t)
            dstt = wqfcT.ap()[dc, :, q4 * 2048:(q4 + 1) * 2048]
            P.add("gpsimd", lambda e, _d=dstt, _b=buf, _m=sem_tsp[slot]: e.dma_start(
                out=_d, in_=_b).then_inc(_m.h, 16))
            ts_t = sem_tsp[slot].inc(16)
            tw_ring.mark(ts_t, sem=sem_tsp[slot])
            tw_cnt[0] += 1

    pj_ring = Ring(2, sem_pjt)
    pj_cnt = [0]

    def emit_pj_transpose(ht):
        slot = pj_cnt[0] % 2
        buf = qwTs[:, 4096 + slot * 2048: 4096 + (slot + 1) * 2048]
        pj_ring.guard(P, "scalar")
        if ht == 0:
            P.wait("scalar", sem_cc, agt["pjA"])
        if ht == 32:
            P.wait("scalar", sem_cc, agt["pjB"])
        wf = wqpj_fullA if ht < 32 else wqpj_fullB
        src = wf.ap()[:, (ht % 32) * 128:((ht % 32) + 1) * 128]
        P.add("scalar", lambda e, _s=src, _b=buf, _m=sem_pwp[slot]: e.dma_start(
            out=_b, in_=_s, transpose=True).then_inc(_m.h, 16))
        tw_t = sem_pwp[slot].inc(16)
        P.wait("gpsimd", sem_pwp[slot], tw_t)
        dstt = wqpjT.ap()[ht, :, :]
        P.add("gpsimd", lambda e, _d=dstt, _b=buf: e.dma_start(
            out=_d, in_=_b).then_inc(sem_pjt.h, 16))
        ts_t = sem_pjt.inc(16)
        pj_ring.mark(ts_t)
        pj_cnt[0] += 1

    def maybe_emit_qdq():
        if qdq_next[0] < 16:
            qu = qdq_next[0]
            emit_qdq_unit(qu)
            if qu >= 8:
                for dc in range(2 * (qu - 8), 2 * (qu - 8) + 2):
                    emit_fc_transpose(dc)
            qdq_next[0] += 1

    unit = 0
    for qc in range(8):
        for h in range(4):
            nkt = 4 * qc + 4
            pvslot = unit % 2
            pv = pbank[2 + pvslot]
            sums = pbank[4 + pvslot][0:1, :]
            qslice = qh[:, h * T + qc * 512: h * T + (qc + 1) * 512]

            r_pv.guard(P, "tensor")
            r_sums.guard(P, "tensor")

            exp_t = []
            msk_t = []

            def emit_pvs(kt, _pv=pv, _sums=sums, _nkt=nkt, _qc=qc):
                P.wait("tensor", sem_pr, exp_t[kt])
                if msk_t[kt] is not None:
                    P.wait("tensor", sem_pm, msk_t[kt])
                pslot = prob_slots[kt]
                pb = probsb[:, pslot * 512:(pslot + 1) * 512]
                P.add("tensor", lambda e, _kt=kt, _pb=pb: e.matmul(
                    _pv[:, :], vt[:, _kt * 128:(_kt + 1) * 128], _pb,
                    start=(_kt == 0), stop=(_kt == _nkt - 1)))
                P.add("tensor", lambda e, _kt=kt, _pb=pb: e.matmul(
                    _sums, ones_bf[:, :], _pb,
                    start=(_kt == 0), stop=(_kt == _nkt - 1)).then_inc(sem_pv.h, 1))
                sem_pv.inc(1)
                pvs_t.append(sem_pv.n)

            prob_slots = []
            for kt in range(nkt):
                lslot = r_lg.slot()
                lg = pbank[lslot]
                r_lg.guard(P, "tensor")
                P.add("tensor", lambda e, _kt=kt, _lg=lg, _q=qslice: e.matmul(
                    _lg[:, :], kh[:, _kt * 128:(_kt + 1) * 128], _q,
                    start=True, stop=True).then_inc(sem_qk.h, 1))
                qkt = sem_qk.inc(1)

                P.wait("scalar", sem_qk, qkt)
                g = gexp[0]
                gexp[0] += 1
                pslot = g % 3
                prob_slots.append(pslot)
                if g >= 3:
                    P.wait("scalar", sem_pv, pvs_t[g - 3])
                pb = probsb[:, pslot * 512:(pslot + 1) * 512]
                P.add("scalar", lambda e, _lg=lg, _pb=pb: e.activation(
                    _pb, _lg[:, :], AF.Exp, bias=mcore[:, :], scale=ISQ).then_inc(sem_pr.h, 1))
                exp_t.append(sem_pr.inc(1))
                r_lg.mark()

                if kt >= 4 * qc:
                    P.wait("vector", sem_pr, exp_t[-1])
                    dk_ = kt - 4 * qc
                    P.add("vector", lambda e, _pb=pb, _dk=dk_: e.tensor_mul(
                        _pb, _pb, trims[:, _dk * 512:(_dk + 1) * 512]).then_inc(sem_pm.h, 1))
                    msk_t.append(sem_pm.inc(1))
                else:
                    msk_t.append(None)

                if kt >= 1:
                    emit_pvs(kt - 1)
            emit_pvs(nkt - 1)
            pv_done = sem_pv.n

            P.wait("vector", sem_pv, pv_done)
            r_rdr.guard(P, "vector")
            rrow = rdrow[:, 0:512]
            P.add("vector", lambda e, _s=sums, _r=rrow: e.reciprocal_approx_fast(_r, _s).then_inc(sem_rd.h, 1))
            rd_t = sem_rd.inc(1)
            r_sums.mark(rd_t)

            P.wait("tensor", sem_rd, rd_t)
            rdx = pbank[6 + pvslot]
            r_rdx.guard(P, "tensor")
            P.add("tensor", lambda e, _r=rrow, _o=rdx: e.matmul(
                _o[:, :], onesrow32[:, :], _r, start=True, stop=True).then_inc(sem_bc2.h, 1))
            bc2_t = sem_bc2.inc(1)
            r_rdr.mark(bc2_t)

            P.wait("scalar", sem_pv, pv_done)
            r_pvt.guard(P, "scalar")
            pvt = pvtmp[:, 0:512]
            P.add("scalar", lambda e, _pv=pv, _t=pvt: e.activation(
                _t, _pv[:, :], AF.Copy).then_inc(sem_pt.h, 1))
            pt_t = sem_pt.inc(1)
            r_pv.mark(pt_t)

            P.wait("vector", sem_pt, pt_t)
            P.wait("vector", sem_bc2, bc2_t)
            ahsl = ah[:, h * T + qc * 512: h * T + (qc + 1) * 512]
            P.add("vector", lambda e, _t=pvt, _x=rdx, _a=ahsl: e.tensor_mul(
                _a, _t, _x[:, :]).then_inc(sem_ev.h, 1))
            ev_t = sem_ev.inc(1)
            r_rdx.mark(ev_t)
            r_pvt.mark(ev_t)
            if unit % 2 == 1:
                maybe_emit_qdq()
            unit += 1

        P.wait("tensor", sem_ev, sem_ev.n)
        if qc == 0:
            P.wait("tensor", sem_cw, TRI_T)
        for ot in range(16):
            cslot = r_lg.slot()
            cps = pbank[cslot]
            r_lg.guard(P, "tensor")
            for fh in range(4):
                last = fh == 3
                P.add("tensor", lambda e, _f=fh, _o=ot, _c=cps, _qc=qc, _l=last: (
                    e.matmul(_c[:, :], cws[:, _f * 2048 + _o * 128: _f * 2048 + _o * 128 + 128],
                             ah[:, _f * T + _qc * 512: _f * T + (_qc + 1) * 512],
                             start=(_f == 0), stop=(_f == 3)).then_inc(sem_cp.h, 1)
                    if _l else
                    e.matmul(_c[:, :], cws[:, _f * 2048 + _o * 128: _f * 2048 + _o * 128 + 128],
                             ah[:, _f * T + _qc * 512: _f * T + (_qc + 1) * 512],
                             start=(_f == 0), stop=(_f == 3))))
            cp_t = sem_cp.inc(1)

            P.wait("vector", sem_cp, cp_t)
            sslot = r_sph.slot()
            r_sph.guard(P, "vector")
            sph = sparth[:, sslot * 512:(sslot + 1) * 512]
            P.add("vector", lambda e, _c=cps, _s=sph: e.tensor_copy(
                out=_s, in_=_c[:, :]).then_inc(sem_ce.h, 1))
            ce_t = sem_ce.inc(1)
            r_lg.mark(ce_t, sem=sem_ce)

            P.wait("gpsimd", sem_ce, ce_t)
            dst = spart_dram.ap()[qc // 2, ot, :, (qc % 2) * 512:(qc % 2) * 512 + 512]
            _ssp = sem_spp[sslot]
            P.add("gpsimd", lambda e, _d=dst, _s=sph, _m=_ssp: e.dma_start(
                out=_d, in_=_s).then_inc(_m.h, 16))
            sp_t = _ssp.inc(16)
            r_sph.mark(sp_t, sem=_ssp)

    while qdq_next[0] < 16:
        maybe_emit_qdq()
    P.wait("gpsimd", sem_spp[0], sem_spp[0].n)
    P.wait("gpsimd", sem_spp[1], sem_spp[1].n)
    P.add("gpsimd", lambda e: e.collective_compute(
        "ReduceScatter", OP.add,
        replica_groups=[[0, 1, 2, 3], [4, 5, 6, 7]],
        ins=[spart_dram.ap().opt()], outs=[sown_dram.ap().opt()],
    ).then_inc(sem_cc.h, 1))
    RS_T = sem_cc.inc(1)


    sem_fcd = S("sem_fcd")
    sem_fq = S("sem_fq")
    sem_hv = S("sem_hv")
    sem_pj = S("sem_pj")
    sem_g = S("sem_g")
    sem_go = S("sem_go")

    relub = vwTs
    sb_s = qwTs
    sb_x = arenaA[:, 0:2048]
    outb = arenaA[:, 2048:4096]
    wstream = arenaA[:, 4096:8192].bitcast(F16)

    def hsq_slice(ht, th):
        if ht < 32:
            return big1[:, ht * 512:(ht + 1) * 512]
        return qh[:, (ht - 32) * 512:(ht - 31) * 512]

    r_wt = Ring(2, sem_fcd)
    r_fcp = Ring(4, sem_hv)
    r_pjp = Ring(4, sem_g)
    r_relu = Ring(2, sem_hv)
    r_gb = Ring(2, sem_go)
    r_sx = Ring(2, sem_go)
    r_outb = Ring(2, sem_out)

    for _eng in ("tensor", "vector", "scalar", "sync"):
        for _sem in (sem_pr, sem_pm, sem_pv, sem_rd, sem_bc2, sem_pt, sem_ev, sem_cp,
                     sem_ce, sem_qv, sem_qs, sem_qu, sem_qw, sem_spp[0], sem_spp[1],
                     sem_tsp[0], sem_tsp[1]):
            P.wait(_eng, _sem, _sem.n)

    for th in range(2):
        for hg in range(16):
            for dcb in range(2):
                wslot = r_wt.slot()
                r_wt.guard(P, "sync")
                wsb = wstream[:, wslot * 4096:(wslot + 1) * 4096]
                _sw = sem_wtp[wslot]
                src = wqfcT.ap()[dcb * 8:(dcb + 1) * 8, :,
                                 hg * 512:(hg + 1) * 512].transpose([1, 0, 2])
                P.add("sync", lambda e, _src=src, _w=wsb, _s=_sw: e.dma_start(
                    out=_w.rearrange("p (c q) -> p c q", c=8), in_=_src).then_inc(_s.h, 16))
                wt_t = _sw.inc(16)

                P.wait("tensor", _sw, wt_t)
                if dcb == 0 and hg >= 1:
                    P.wait("tensor", sem_hv, r_fcp.marks[-1])
                for dci in range(8):
                    dc = dcb * 8 + dci
                    for j in range(4):
                        last = (dci == 7 and j == 3)
                        P.add("tensor", lambda e, _j=j, _dci=dci, _dc=dc, _th=th, _w=wsb, _l=last: (
                            e.matmul(pbank[_j][:, :], _w[:, _dci * 512 + _j * 128:_dci * 512 + _j * 128 + 128],
                                     xnow[:, _dc * 1024 + _th * 512:_dc * 1024 + _th * 512 + 512],
                                     start=(_dc == 0), stop=(_dc == 15)).then_inc(sem_fcd.h, 1)
                            if _l else
                            e.matmul(pbank[_j][:, :], _w[:, _dci * 512 + _j * 128:_dci * 512 + _j * 128 + 128],
                                     xnow[:, _dc * 1024 + _th * 512:_dc * 1024 + _th * 512 + 512],
                                     start=(_dc == 0), stop=(_dc == 15))))
                fcd_t = sem_fcd.inc(1)
                r_wt.mark(fcd_t)
            P.wait("scalar", sem_fcd, fcd_t)
            for j in range(4):
                rslot = r_relu.slot()
                r_relu.guard(P, "scalar")
                rb = relub[:, rslot * 512:(rslot + 1) * 512]
                hs = hsq_slice(hg * 4 + j, th)
                P.add("scalar", lambda e, _j=j, _r=rb: e.activation(_r, pbank[_j][:, :], AF.Relu))
                P.add("scalar", lambda e, _r=rb, _h=hs: e.activation(
                    _h, _r, AF.Square).then_inc(sem_hv.h, 1))
                hv_t = sem_hv.inc(1)
                r_relu.mark(hv_t)
                r_fcp.mark(hv_t)
            if th == 0:
                for _k in range(4):
                    emit_pj_transpose(hg * 4 + _k)

        P.wait("tensor", sem_hv, sem_hv.n)
        if th == 0:
            P.wait("gpsimd", sem_cc, RS_T)
        for og in range(4):
            for hc in range(8):
                wslot = r_wt.slot()
                r_wt.guard(P, "sync")
                wsb = wstream[:, wslot * 4096:(wslot + 1) * 4096]
                _sw = sem_wtp[wslot]
                if th == 0 and og == 0:
                    P.wait("sync", sem_pjt, 16 * (8 * hc + 8))
                src = wqpjT.ap()[hc * 8:(hc + 1) * 8, :,
                                 og * 512:(og + 1) * 512].transpose([1, 0, 2])
                P.add("sync", lambda e, _src=src, _w=wsb, _s=_sw: e.dma_start(
                    out=_w.rearrange("p (c q) -> p c q", c=8), in_=_src).then_inc(_s.h, 16))
                wt_t = _sw.inc(16)
                P.wait("tensor", _sw, wt_t)
                if hc == 0 and (th > 0 or og >= 1):
                    P.wait("tensor", sem_g, r_pjp.marks[-1])
                for hti in range(8):
                    ht = hc * 8 + hti
                    for j in range(4):
                        last = (hti == 7 and j == 3)
                        P.add("tensor", lambda e, _j=j, _hti=hti, _ht=ht, _th=th, _w=wsb, _l=last: (
                            e.matmul(pbank[4 + _j][:, :], _w[:, _hti * 512 + _j * 128:_hti * 512 + _j * 128 + 128],
                                     hsq_slice(_ht, _th),
                                     start=(_ht == 0), stop=(_ht == 63)).then_inc(sem_fcd.h, 1)
                            if _l else
                            e.matmul(pbank[4 + _j][:, :], _w[:, _hti * 512 + _j * 128:_hti * 512 + _j * 128 + 128],
                                     hsq_slice(_ht, _th),
                                     start=(_ht == 0), stop=(_ht == 63))))
                fcd_t = sem_fcd.inc(1)
                r_wt.mark(fcd_t)
            for j in range(4):
                ot = og * 4 + j
                sxslot = r_sx.slot()
                r_sx.guard(P, "gpsimd")
                s_t = sb_s[:, sxslot * 512:(sxslot + 1) * 512]
                x_t = sb_x[:, sxslot * 512:(sxslot + 1) * 512]
                _ssx = sem_sxp[sxslot]
                P.add("gpsimd", lambda e, _ot=ot, _th=th, _s=s_t, _m=_ssx: e.dma_start(
                    out=_s, in_=sown_dram.ap()[_ot, :, _th * 512:(_th + 1) * 512]).then_inc(_m.h, 16))
                P.add("gpsimd", lambda e, _ot=ot, _th=th, _x=x_t, _m=_ssx: e.dma_start(
                    out=_x, in_=x_ownT[_ot * 128:(_ot + 1) * 128, _th * 512:(_th + 1) * 512]).then_inc(_m.h, 16))
                sx_t = _ssx.inc(32)

                P.wait("scalar", sem_fcd, fcd_t)
                gslot = r_gb.slot()
                r_gb.guard(P, "scalar")
                g_t = relub[:, (2 + gslot) * 512:(3 + gslot) * 512]
                P.add("scalar", lambda e, _j=j, _g=g_t: e.activation(
                    _g, pbank[4 + _j][:, :], AF.Silu).then_inc(sem_g.h, 1))
                g_tt = sem_g.inc(1)
                r_pjp.mark(g_tt)

                P.wait("vector", sem_g, g_tt)
                P.wait("vector", _ssx, sx_t)
                oslot = r_outb.slot()
                r_outb.guard(P, "vector")
                o_t = outb[:, oslot * 512:(oslot + 1) * 512]
                P.add("vector", lambda e, _s=s_t, _g=g_t, _o=o_t: e.tensor_mul(_o, _s, _g))
                P.add("vector", lambda e, _x=x_t, _o=o_t: e.tensor_add(
                    _o, _o, _x).then_inc(sem_go.h, 1))
                go_t = sem_go.inc(1)
                r_gb.mark(go_t)
                r_sx.mark(go_t)

                P.wait("gpsimd", sem_go, go_t)
                ost = dma("gpsimd", out[ot * 128:(ot + 1) * 128, th * 512:(th + 1) * 512], o_t,
                          sem_outp[oslot])
                r_outb.mark(ost, sem=sem_outp[oslot])

    P.wait("gpsimd", sem_outp[0], sem_outp[0].n)
    P.wait("gpsimd", sem_outp[1], sem_outp[1].n)
    if debug:
        dma("gpsimd", dbg["qh"], qh[:, :], sem_out)
        dma("gpsimd", dbg["kh"], kh[:, :], sem_out)
        dma("gpsimd", dbg["vt"], vt[:, :], sem_out)
        dma("gpsimd", dbg["xn"], xnow[:, :], sem_out)
        dma("gpsimd", dbg["mc"], mrow[:, 0:8], sem_out)
        P.wait("gpsimd", sem_ev, sem_ev.n)
        dma("gpsimd", dbg["ah"], ah[:, :], sem_out)
        P.wait("gpsimd", sem_cc, RS_T)
        dma("gpsimd", dbg["sown"], sown_dram.ap(), sem_out)
        dma("gpsimd", dbg["wqfc"], wqfc_full.ap(), sem_out)
        dma("gpsimd", dbg["wqpj"][:, 0:4096], wqpj_fullA.ap(), sem_out)
        dma("gpsimd", dbg["wqpj"][:, 4096:8192], wqpj_fullB.ap(), sem_out)
        dma("gpsimd", dbg["spart"], spart_dram.ap(), sem_out)
    P.wait("gpsimd", sem_out, sem_out.n)

    with nc.Block() as block:
        P.run(block)
    stack.close()
    mybir.codegen_inst_isa_subclasses(nc)
    return nc


_NC_CACHE = {}


def _get_nc(debug=DEBUG):
    key = ("v1", debug)
    if key not in _NC_CACHE:
        _NC_CACHE[key] = build_program(debug)
    return _NC_CACHE[key]


def kernel(L0_attn_x, L0_attn_q_w, L0_attn_kv_w, L0_attn_c_w, L0_ffn_w_fc, L0_ffn_w_proj,
           _debug=DEBUG, _return_raw=False):
    x = np.asarray(L0_attn_x, np.float32)
    q_w = np.asarray(L0_attn_q_w, np.float32)
    kv_w = np.asarray(L0_attn_kv_w, np.float32)
    c_w = np.asarray(L0_attn_c_w, np.float32)
    w_fc = np.asarray(L0_ffn_w_fc, np.float32)
    w_proj = np.asarray(L0_ffn_w_proj, np.float32)

    C2t, S2t = _rope_tables()
    perm = _perm128()
    trim = _tri_masks()
    in_maps = [_prep_core_inputs(c, x, q_w, kv_w, c_w, w_fc, w_proj, C2t, S2t, perm, trim)
               for c in range(8)]
    nc = _get_nc(_debug)
    res = run_bass_kernel_spmd(nc, in_maps, core_ids=list(range(8)))
    if _return_raw:
        return res
    outp = np.empty((2, 4096, 2048), np.float32)
    for c in range(8):
        b, r = c // 4, c % 4
        o = res.results[c]["out"]
        outp[b, r * 1024:(r + 1) * 1024, :] = o.T
    return outp
